# version 01
# baseline (speedup 1.0000x reference)
import numpy as np
import ml_dtypes
from contextlib import ExitStack

import concourse.bass as bass
import concourse.mybir as mybir
from concourse.bass_utils import run_bass_kernel_spmd

F32 = mybir.dt.float32
BF16 = mybir.dt.bfloat16
U8 = mybir.dt.uint8
AF = mybir.ActivationFunctionType
ALU = mybir.AluOpType
AX = mybir.AxisListType

BF16NP = ml_dtypes.bfloat16

DM = 4096
H, PD, G, ND = 128, 64, 8, 128
DI = H * PD
CONV = DI + 2 * G * ND
PROJ = DI + CONV + H
T = 2048
PFX = 16
TE = T + PFX
NCH = 16
CH = 128
EPS = 1e-5
REP = H // G

OT_HID = list(range(64, 128))
OT_B = list(range(128, 136))
OT_C = list(range(136, 144))
OT_GATE = list(range(0, 64))
OT_DT = [144]
TCHUNKS = [(0, 16), (16, 512), (528, 512), (1040, 512), (1552, 512)]

BUILD_UPTO = 11
DEBUG = False

KB = 1024
SEG_B = 31 * KB
SEG_C = 71 * KB + 512
ARENA = 205 * KB


class Sem:
    def __init__(self, nc, ctx, name):
        self.h = ctx.enter_context(nc.semaphore(name))
        self.n = 0

    def inc(self, instr, k=1):
        instr.then_inc(self.h, k)
        self.n += k
        return self.n


class DmaG:

    def __init__(self, nc, ctx, name, k):
        self.sems = [Sem(nc, ctx, f"{name}{j}") for j in range(k)]
        self.tokens = []

    def dma(self, eng, out, in_, **kw):
        s = self.sems[len(self.tokens) % len(self.sems)]
        if s.n:
            eng.wait_ge(s.h, s.n)
        ins = eng.dma_start(out=out, in_=in_, **kw)
        s.inc(ins, 16)
        self.tokens.append((s, s.n))
        return len(self.tokens) - 1

    def wait(self, eng, tok):
        s, v = self.tokens[tok]
        eng.wait_ge(s.h, v)

    def wait_all(self, eng):
        for s in self.sems:
            if s.n:
                eng.wait_ge(s.h, s.n)


def build(debug=False, upto=BUILD_UPTO):
    nc = bass.Bass("TRN2", target_bir_lowering=False)

    def din(name, shape, dtype=F32):
        return nc.dram_tensor(name, shape, dtype, kind="ExternalInput").ap()

    x_own = din("x_own", [TE, DM])
    x_dup = din("x_dup", [TE, DM])
    mask_in = din("mask_in", [128, 1])
    wipT = din("wipT", [DM, PROJ])
    wopT = din("wopT", [DI, DM])
    conv_w_in = din("conv_w_r", [128, 80 * 4])
    conv_b_in = din("conv_b_r", [128, 80])
    dt_bias_in = din("dt_bias", [128, 1])
    a_log_in = din("a_log", [128, 1])
    dp_row_in = din("dp_row", [1, 128])
    gnw_in = din("gnw_col", [128, 64])
    nw_row_in = din("nw_row", [1, DM])
    vhT = din("vhT", [DM, 4032])
    utT = din("utT", [4032, DM])
    vhtopT = din("vhtopT", [DM, 64])
    uttopT_in = din("uttopT", [64, DM])
    vhrT = din("vhrT", [DM, 4])
    urT = din("urT", [4, DM])
    r_in = din("r_mat", [4, 4])
    stail_in = din("stail_col", [128, 32])
    stop_in = din("stop_col", [64, 1])
    alpha_in = din("alpha_col", [64, 1])
    beta_in = din("beta_col", [64, 1])
    str_in = din("str_col", [4, 1])
    dvec_in = din("dvec_col", [4, 1])
    fbias_in = din("fbias_row", [1, DM])

    out_t = nc.dram_tensor("out", [T, DM], F32, kind="ExternalOutput").ap()

    def scr(name, shape, dtype):
        return nc.dram_tensor(name, shape, dtype, kind="Internal").ap()

    h_scr = scr("h_scr", [TE, DM], BF16)
    hid_scr = scr("hid_scr", [DI, T], BF16)
    b_scr = scr("b_scr", [G * ND, T], BF16)
    c_scr = scr("c_scr", [G * ND, T], BF16)
    gate_scr = scr("gate_scr", [DI, T], BF16)
    acs_scr = scr("acs_scr", [128, 128], F32)
    p_scr = scr("p_scr", [128, DI], F32)
    y2_scr = scr("y2_scr", [T, DI], BF16)
    mo_scr = scr("mo_scr", [T, DM], F32)
    u_scr = scr("u_scr", [T, DM], F32)
    v_scr = scr("v_scr", [T, DM], BF16)
    mid_scr = scr("mid_scr", [4096, T], BF16)
    mdl_scr = scr("mdl_scr", [4, T], BF16)
    ffn_scr = scr("ffn_scr", [T, DM], F32)

    dbg_outs = {}
    if debug:
        for nm, shape, dt_ in [
            ("d_h", [TE, DM], BF16), ("d_hid", [DI, T], BF16),
            ("d_b", [G * ND, T], BF16), ("d_dt", [128, T], F32),
            ("d_da", [128, T], F32), ("d_p", [128, DI], F32),
            ("d_nwbc", [128, DM], F32), ("d_y2", [T, DI], BF16),
            ("d_c", [G * ND, T], BF16), ("d_gate", [DI, T], BF16),
            ("d_mo", [T, DM], F32), ("d_u", [T, DM], F32),
            ("d_v", [T, DM], BF16), ("d_mid", [4096, T], BF16),
            ("d_mdl", [4, T], BF16), ("d_ffn", [T, DM], F32),
        ]:
            dbg_outs[nm] = nc.dram_tensor(nm, shape, dt_, kind="ExternalOutput").ap()

    jj, ii = np.meshgrid(np.arange(128), np.arange(128), indexing="ij")
    ident_bf = nc.inline_tensor(np.eye(128, dtype=BF16NP), name="ident_bf").ap()
    ident_f = nc.inline_tensor(np.eye(128, dtype=np.float32), name="ident_f").ap()
    ut_tri = nc.inline_tensor((jj <= ii).astype(np.float32), name="ut_tri").ap()
    su_tri = nc.inline_tensor((jj > ii).astype(np.float32), name="su_tri").ap()
    tri01 = nc.inline_tensor((ii >= jj).astype(np.float32), name="tri01").ap()
    allones = nc.inline_tensor(np.ones((128, 128), np.float32), name="allones").ap()
    onesrow = nc.inline_tensor(np.ones((1, 128), np.float32), name="onesrow").ap()

    est = ExitStack()
    with est:
        arena = est.enter_context(nc.sbuf_tensor("arena", [128, ARENA], U8))
        AR = arena[:, :]
        regions = {}

        def reg(name, off, nbytes):
            assert off % 32 == 0 and off + nbytes <= ARENA, (name, off, nbytes)
            regions[name] = (off, nbytes)

        def ap(name, dtype, rows, cols, row0=0, col0=0):
            off, nb = regions[name]
            esz = mybir.dt.size(dtype)
            b0 = off + col0 * esz
            assert b0 + cols * esz <= off + nb, (name, col0, cols, nb)
            return AR[row0:row0 + rows, b0:b0 + cols * esz].bitcast(dtype)

        def h3(x):
            return x.rearrange("p (h o) -> p h o", o=64)

        def w3(x, n=128):
            return x.rearrange("p (h o) -> p h o", o=1).broadcast_to([128, n, 64])

        po = 0

        def pers(name, nbytes):
            nonlocal po
            reg(name, po, nbytes)
            po = (po + nbytes + 31) & ~31
        pers("ibf", 128 * 2)
        pers("idf", 128 * 4)
        pers("cut", 128 * 4)
        pers("csu", 128 * 4)
        pers("ctr", 128 * 4)
        pers("c1s", 128 * 4)
        pers("c1r", 128 * 4)
        pers("cw", 320 * 4)
        pers("cb", 80 * 4)
        pers("dtb", 32)
        pers("acol", 32)
        pers("mskc", 32)
        pers("dpbc", 128 * 4)
        pers("gnw", 64 * 4)
        pers("stail", 32 * 4)
        pers("shead", 32)
        pers("sdc", 32)
        pers("w2t", DM * 2)
        pers("epsc", 32)
        pers("col", 64 * 4)
        pers("col2", 64 * 4)
        assert po <= 15 * KB, po

        reg("dtfm", 15 * KB, T * 4)
        reg("dafm", 23 * KB, T * 4)

        reg("hp_x0", SEG_B, DM * 4)
        reg("hp_x1", SEG_B + 16 * KB, DM * 4)
        reg("hp_hbf", SEG_B + 32 * KB, DM * 2)
        reg("ip_w0", SEG_B, 32 * 128 * 2)
        reg("ip_w1", SEG_B + 8 * KB, 32 * 128 * 2)
        reg("ip_u", SEG_B + 16 * KB, TE * 4)
        reg("ip_acc", SEG_B + 24 * KB + 512, T * 4)
        reg("ip_c0", SEG_B + 32 * KB + 512, T * 2)
        reg("ip_c1", SEG_B + 36 * KB + 512, T * 2)
        reg("ip_t1", SEG_B + 24 * KB + 512, T * 4)
        reg("ip_t2", SEG_B + 32 * KB + 512, T * 4)
        reg("sc_btm", SEG_B, 8 * 128 * 2)
        reg("sc_bfm", SEG_B + 2 * KB, 8 * 128 * 2)
        reg("sc_cfm", SEG_B + 4 * KB, 8 * 128 * 2)
        reg("sc_datm", SEG_B + 6 * KB, 128 * 4)
        reg("sc_dttm", SEG_B + 6 * KB + 512, 128 * 4)
        reg("sc_dec", SEG_B + 7 * KB, 128 * 4)
        reg("sc_etb", SEG_B + 7 * KB + 512, 128 * 4)
        reg("sc_acs", SEG_B + 8 * KB, 128 * 4)
        reg("sc_eacs", SEG_B + 8 * KB + 512, 128 * 4)
        reg("sc_w2", SEG_B + 9 * KB, 128 * 4)
        reg("sc_acsf", SEG_B + 9 * KB + 512, 128 * 4)
        reg("sc_row", SEG_B + 10 * KB, 2048 * 4)
        reg("sc_dsb", SEG_B + 18 * KB, 1024 * 4)
        reg("sc_sqj", SEG_B + 34 * KB, 512 * 4)
        reg("sc_gm", SEG_B + 22 * KB, 128 * 4)
        reg("sc_mt", SEG_B + 23 * KB, 8 * 128 * 2)
        reg("sc_t1", SEG_B + 25 * KB, 512 * 4)
        reg("sc_sg", SEG_B + 27 * KB, 512 * 4)
        reg("sc_yg", SEG_B + 29 * KB, 1024 * 4)
        reg("sc_dph", SEG_B + 33 * KB, 512 * 2)
        reg("f1_w0", SEG_B, 32 * 128 * 2)
        reg("f1_w1", SEG_B + 8 * KB, 32 * 128 * 2)
        reg("f1_md0", SEG_B + 16 * KB, T * 2)
        reg("f1_md1", SEG_B + 20 * KB, T * 2)
        reg("f2_ut", SEG_B, 32 * 512 * 2)
        reg("f2_st", SEG_B + 32 * KB, 512 * 4)
        reg("f2_bb", SEG_B + 34 * KB, 512 * 4)
        reg("f2_s1", SEG_B + 36 * KB, 512 * 4)

        reg("rowtmp", SEG_C, DM * 4)
        reg("nwbc", SEG_C + 16 * KB, DM * 4)
        reg("hfm", SEG_C, 32 * TE * 2)
        reg("sc_hid", SEG_C, DI * 2)
        reg("sc_hd2big", SEG_C + 16 * KB, DI * 2)
        reg("pst", SEG_C + 32 * KB, DI * 4)
        reg("sc_gate", SEG_C + 64 * KB, DI * 2)
        reg("sc_hdbig", SEG_C + 80 * KB, DI * 2)
        reg("sc_y2", SEG_C + 96 * KB, DI * 2)
        reg("sc_pbf", SEG_C + 112 * KB, DI * 2)
        reg("ur_f", SEG_C + 32 * KB, DM * 4)
        reg("op_ya", SEG_C, 1024 * 2)
        reg("op_yb", SEG_C + 2 * KB, 1024 * 2)
        reg("op_s0", SEG_C + 4 * KB, 512 * 4)
        reg("op_s1", SEG_C + 6 * KB, 512 * 4)
        reg("op_w", SEG_C + 64 * KB, 64 * 512 * 2)
        reg("me_x0", SEG_C, DM * 4)
        reg("me_x1", SEG_C + 16 * KB, DM * 4)
        reg("me_mo0", SEG_C + 32 * KB, DM * 4)
        reg("me_mo1", SEG_C + 48 * KB, DM * 4)
        reg("me_u", SEG_C + 64 * KB, DM * 4)
        reg("me_v", SEG_C + 80 * KB, DM * 2)
        reg("me_j", SEG_C + 88 * KB, DM * 2)
        reg("vfm", SEG_C, 32 * T * 2)
        reg("mid", SEG_C, 32 * T * 2)
        reg("f2_mdl", SEG_C + 128 * KB, T * 2)
        reg("fn_f0", SEG_C, DM * 4)
        reg("fn_f1", SEG_C + 16 * KB, DM * 4)
        reg("fn_u0", SEG_C + 32 * KB, DM * 4)
        reg("fn_u1", SEG_C + 48 * KB, DM * 4)
        reg("fn_j", SEG_C + 64 * KB, DM * 2)

        pbs = [est.enter_context(nc.psum_tensor(f"pb{i}", [128, 512], F32))
               for i in range(8)]

        gX = DmaG(nc, est, "gX", 2)
        gW = DmaG(nc, est, "gW", 2)
        gSP = DmaG(nc, est, "gSP", 4)
        gTD = DmaG(nc, est, "gTD", 4)
        gOUT = DmaG(nc, est, "gOUT", 2)
        sPE = Sem(nc, est, "sPE")
        sDV = Sem(nc, est, "sDV")
        sAC = Sem(nc, est, "sAC")
        ALL_SEMS = [sPE, sDV, sAC]
        ALL_GRPS = [gX, gW, gSP, gTD, gOUT]

        block = est.enter_context(nc.Block())

        def barrier(eng):
            for s_ in ALL_SEMS:
                if s_.n:
                    eng.wait_ge(s_.h, s_.n)
            for g_ in ALL_GRPS:
                g_.wait_all(eng)

        def snapshot():
            snap = [(s_.h, s_.n) for s_ in ALL_SEMS]
            for g_ in ALL_GRPS:
                snap += [(s_.h, s_.n) for s_ in g_.sems]
            return snap

        def wait_snap(eng, snap):
            for h, n in snap:
                if n:
                    eng.wait_ge(h, n)

        def phase0():
            @block.sync
            def _(sy):
                for dst, src in [
                    (ap("ibf", BF16, 128, 128), ident_bf),
                    (ap("idf", F32, 128, 128), ident_f),
                    (ap("cut", F32, 128, 128), ut_tri),
                    (ap("csu", F32, 128, 128), su_tri),
                    (ap("ctr", F32, 128, 128), tri01),
                    (ap("c1s", F32, 128, 128), allones),
                    (ap("c1r", F32, 1, 128), onesrow),
                    (ap("cw", F32, 128, 320), conv_w_in),
                    (ap("cb", F32, 128, 80), conv_b_in),
                    (ap("dtb", F32, 128, 1), dt_bias_in),
                    (ap("col", F32, 128, 1), a_log_in),
                    (ap("mskc", F32, 128, 1), mask_in),
                    (ap("gnw", F32, 128, 64), gnw_in),
                    (ap("stail", F32, 128, 32), stail_in),
                    (ap("rowtmp", F32, 1, 128), dp_row_in),
                    (ap("col2", F32, 64, 1), stop_in),
                    (ap("col2", F32, 64, 1, col0=1), alpha_in),
                    (ap("col2", F32, 64, 1, col0=2), beta_in),
                    (ap("col2", F32, 4, 1, col0=3), str_in),
                    (ap("col2", F32, 4, 1, col0=4), dvec_in),
                    (ap("col2", F32, 4, 4, col0=5), r_in),
                    (ap("ur_f", F32, 4, DM), urT),
                ]:
                    gX.dma(sy, out=dst, in_=src)

            @block.scalar
            def _(s):
                gX.wait_all(s)
                ins = s.activation(ap("acol", F32, 128, 1), ap("col", F32, 128, 1),
                                   AF.Exp)
                sAC.inc(ins)

            @block.vector
            def _(v):
                gX.wait_all(v)
                v.wait_ge(sAC.h, sAC.n)
                v.memset(ap("epsc", F32, 128, 1), EPS)
                v.tensor_scalar(ap("acol", F32, 128, 1), ap("acol", F32, 128, 1),
                                -1.0, None, ALU.mult)
                ins = v.tensor_scalar(ap("shead", F32, 64, 1), ap("col2", F32, 64, 1),
                                      ap("col2", F32, 64, 1, col0=1),
                                      ap("col2", F32, 64, 1, col0=2),
                                      ALU.mult, ALU.add)
                sDV.inc(ins)
                v.wait_ge(sDV.h, sDV.n)
                ins = v.tensor_scalar(ap("shead", F32, 64, 1), ap("shead", F32, 64, 1),
                                      0.0, None, ALU.max)
                sDV.inc(ins)
                v.wait_ge(sDV.h, sDV.n)
                ins = v.tensor_scalar(ap("sdc", F32, 4, 1),
                                      ap("col2", F32, 4, 1, col0=4),
                                      0.0, None, ALU.max)
                sDV.inc(ins)
                v.wait_ge(sDV.h, sDV.n)
                ins = v.tensor_tensor(ap("sdc", F32, 4, 1), ap("sdc", F32, 4, 1),
                                      ap("col2", F32, 4, 1, col0=3), ALU.mult)
                sDV.inc(ins)
                v.wait_ge(sDV.h, sDV.n)
                ins = v.tensor_scalar(ap("ur_f", F32, 4, DM), ap("ur_f", F32, 4, DM),
                                      ap("sdc", F32, 4, 1), None, ALU.mult)
                sDV.inc(ins)

            pe0 = sPE.n
            dv0 = sDV.n

            @block.tensor
            def _(t):
                t.wait_ge(sDV.h, dv0)
                ins = t.matmul(pbs[0][:, 0:128], ap("c1r", F32, 1, 128),
                               ap("rowtmp", F32, 1, 128), start=True, stop=True)
                sPE.inc(ins)
                for j in range(8):
                    if j >= 2:
                        t.wait_ge(sDV.h, dv0 + j)
                    ins = t.matmul(pbs[2 + j % 2][0:4, 0:512],
                                   ap("col2", F32, 4, 4, col0=5),
                                   ap("ur_f", F32, 4, 512, col0=j * 512),
                                   start=True, stop=True)
                    sPE.inc(ins)

            @block.vector
            def _(v):
                v.wait_ge(sPE.h, pe0 + 1)
                ins = v.tensor_copy(ap("dpbc", F32, 128, 128), pbs[0][:, 0:128])
                sDV.inc(ins)
                for j in range(8):
                    v.wait_ge(sPE.h, pe0 + 2 + j)
                    ins = v.tensor_copy(ap("w2t", BF16, 4, 512, col0=j * 512),
                                        pbs[2 + j % 2][0:4, 0:512])
                    sDV.inc(ins)

        def emit_nwbc():
            pe0 = sPE.n
            dv0 = sDV.n
            toks = []

            @block.sync
            def _(sy):
                barrier(sy)
                toks.append(gX.dma(sy, out=ap("rowtmp", F32, 1, DM), in_=nw_row_in))

            @block.tensor
            def _(t):
                gX.wait(t, toks[0])
                for j in range(8):
                    if j >= 2:
                        t.wait_ge(sDV.h, dv0 + j - 1)
                    ins = t.matmul(pbs[j % 2][:, 0:512], ap("c1r", F32, 1, 128),
                                   ap("rowtmp", F32, 1, 512, col0=j * 512),
                                   start=True, stop=True)
                    sPE.inc(ins)

            @block.vector
            def _(v):
                for j in range(8):
                    v.wait_ge(sPE.h, pe0 + j + 1)
                    ins = v.tensor_copy(ap("nwbc", F32, 128, 512, col0=j * 512),
                                        pbs[j % 2][:, 0:512])
                    sDV.inc(ins)

        def emit_hprep(x_ap):
            emit_nwbc()
            ttiles = [(i * 128, 128) for i in range(16)] + [(2048, 16)]
            a0 = sAC.n
            d0 = sDV.n
            xt = []
            spt = []

            def cc(i, rr=128):
                return ap("col", F32, rr, 1, col0=4 * i)

            @block.sync
            def _(sy):
                barrier(sy)
                for i, (r0, rr) in enumerate(ttiles):
                    if i >= 2:
                        sy.wait_ge(sDV.h, d0 + 8 * (i - 2) + 8)
                    xt.append(gX.dma(sy, out=ap(f"hp_x{i % 2}", F32, rr, DM),
                                     in_=x_ap[r0:r0 + rr, :]))

            @block.gpsimd
            def _(g):
                for i, (r0, rr) in enumerate(ttiles):
                    g.wait_ge(sDV.h, d0 + 8 * i + 8)
                    spt.append(gSP.dma(g, out=h_scr[r0:r0 + rr, :],
                                       in_=ap("hp_hbf", BF16, rr, DM)))

            @block.scalar
            def _(s):
                for i, (r0, rr) in enumerate(ttiles):
                    xv = ap(f"hp_x{i % 2}", F32, rr, DM)
                    gX.wait(s, xt[i])
                    s.wait_ge(sDV.h, d0 + 8 * i + 1)
                    if i >= 1:
                        gSP.wait(s, spt[i - 1])
                    ins = s.activation(ap("hp_hbf", BF16, rr, DM), xv, AF.Square,
                                       accum_out=cc(1, rr))
                    sAC.inc(ins)
                    s.wait_ge(sDV.h, d0 + 8 * i + 3)
                    ins = s.activation(cc(5, rr), cc(4, rr), AF.Ln,
                                       bias=ap("epsc", F32, rr, 1))
                    sAC.inc(ins)
                    s.wait_ge(sAC.h, sAC.n)
                    ins = s.activation(cc(5, rr), cc(5, rr), AF.Exp, scale=-0.5)
                    sAC.inc(ins)
                    s.wait_ge(sDV.h, d0 + 8 * i + 4)
                    ins = s.activation(cc(7, rr), cc(6, rr), AF.Ln,
                                       bias=ap("epsc", F32, rr, 1))
                    sAC.inc(ins)
                    s.wait_ge(sAC.h, sAC.n)
                    ins = s.activation(cc(7, rr), cc(7, rr), AF.Exp, scale=-0.5)
                    sAC.inc(ins)

            @block.vector
            def _(v):
                for i, (r0, rr) in enumerate(ttiles):
                    xv = ap(f"hp_x{i % 2}", F32, rr, DM)
                    gX.wait(v, xt[i])
                    if i >= 1:
                        v.wait_ge(sAC.h, a0 + 5 * (i - 1) + 5)
                    ins = v.tensor_scalar(xv, xv, 1.0, None, ALU.mult, ALU.add,
                                          accum_out=cc(0, rr))
                    sDV.inc(ins)
                    v.wait_ge(sAC.h, a0 + 5 * i + 1)
                    v.tensor_tensor(cc(3, rr), cc(0, rr), cc(0, rr), ALU.mult)
                    ins = v.tensor_scalar(cc(2, rr), cc(1, rr), 1.0 / DM, None,
                                          ALU.mult)
                    sDV.inc(ins)
                    v.wait_ge(sDV.h, d0 + 8 * i + 2)
                    ins = v.scalar_tensor_tensor(cc(4, rr), cc(3, rr),
                                                 -1.0 / (DM * DM), cc(2, rr),
                                                 ALU.mult, ALU.add)
                    sDV.inc(ins)
                    v.wait_ge(sAC.h, a0 + 5 * i + 3)
                    ins = v.tensor_scalar(cc(6, rr), cc(4, rr), cc(5, rr), cc(5, rr),
                                          ALU.mult, ALU.mult)
                    sDV.inc(ins)
                    v.wait_ge(sAC.h, a0 + 5 * i + 5)
                    ins = v.tensor_scalar(cc(8, rr), cc(5, rr), cc(7, rr), None,
                                          ALU.mult)
                    sDV.inc(ins)
                    v.wait_ge(sDV.h, d0 + 8 * i + 5)
                    ins = v.tensor_scalar(cc(9, rr), cc(8, rr), cc(0, rr),
                                          1.0 / DM, ALU.mult, ALU.mult)
                    sDV.inc(ins)
                    v.wait_ge(sDV.h, d0 + 8 * i + 6)
                    ins = v.tensor_scalar(xv, xv, cc(8, rr), cc(9, rr),
                                          ALU.mult, ALU.subtract)
                    sDV.inc(ins)
                    v.wait_ge(sDV.h, d0 + 8 * i + 7)
                    if i >= 1:
                        gSP.wait(v, spt[i - 1])
                    ins = v.tensor_tensor(ap("hp_hbf", BF16, rr, DM), xv,
                                          ap("nwbc", F32, rr, DM), ALU.mult)
                    sDV.inc(ins)

        def emit_inproj(tiles):
            hft = []

            @block.sync
            def _(sy):
                barrier(sy)
                for kt in range(32):
                    hft.append(gTD.dma(
                        sy, out=ap("hfm", BF16, 128, TE, col0=kt * TE),
                        in_=h_scr[:, kt * 128:(kt + 1) * 128], transpose=True))

            pe0 = sPE.n
            a0 = sAC.n
            d0 = sDV.n
            nt = len(tiles)

            def kindof(ot):
                return "dt" if ot in OT_DT else ("gate" if ot < 64 else "conv")

            kinds = [kindof(ot) for ot in tiles]
            acts_per = [6 if k == "conv" else (5 if k == "gate" else 8)
                        for k in kinds]
            act_cum = np.concatenate([[0], np.cumsum(acts_per)]).astype(int)
            dvs_per = [4 if k == "conv" else (1 if k == "gate" else 4)
                       for k in kinds]
            dv_cum = np.concatenate([[0], np.cumsum(dvs_per)]).astype(int)
            sp_per = [1 if k != "dt" else 0 for k in kinds]
            evac_done = []
            for i in range(nt):
                for ci in range(5):
                    evac_done.append(act_cum[i] + ci + 1)
            post_sem = [sAC if k == "conv" else sDV for k in kinds]
            post_cnt = [int(a0 + act_cum[i] + 6) if kinds[i] == "conv"
                        else int(d0 + dv_cum[i] + 1) for i in range(nt)]
            wt = []
            spt = {}

            def spill_dst(ot):
                if ot < 64:
                    return gate_scr[ot * 128:(ot + 1) * 128, :]
                if ot < 128:
                    return hid_scr[(ot - 64) * 128:(ot - 63) * 128, :]
                if ot < 136:
                    return b_scr[(ot - 128) * 128:(ot - 127) * 128, :]
                return c_scr[(ot - 136) * 128:(ot - 135) * 128, :]

            @block.gpsimd
            def _(g):
                gTD.wait_all(g)
                for i, ot in enumerate(tiles):
                    if i >= 2:
                        g.wait_ge(sPE.h, pe0 + 5 * (i - 1))
                    wt.append(gW.dma(
                        g, out=ap(f"ip_w{i % 2}", BF16, 128, 32 * 128),
                        in_=wipT[:, ot * 128:(ot + 1) * 128].rearrange(
                            "(kt p) o -> p kt o", p=128)))
                    if i >= 1 and sp_per[i - 1]:
                        g.wait_ge(post_sem[i - 1].h, post_cnt[i - 1])
                        spt[i - 1] = gSP.dma(
                            g, out=spill_dst(tiles[i - 1]),
                            in_=ap(f"ip_c{(i - 1) % 2}", BF16, 128, T))
                i = nt - 1
                if sp_per[i]:
                    g.wait_ge(post_sem[i].h, post_cnt[i])
                    spt[i] = gSP.dma(g, out=spill_dst(tiles[i]),
                                     in_=ap(f"ip_c{i % 2}", BF16, 128, T))

            @block.tensor
            def _(t):
                gTD.wait_all(t)
                m = 0
                for i, ot in enumerate(tiles):
                    gW.wait(t, wt[i])
                    wv = ap(f"ip_w{i % 2}", BF16, 128, 32 * 128)
                    for (c0, csz) in TCHUNKS:
                        bank = pbs[m % 4]
                        if m >= 4:
                            t.wait_ge(sAC.h, int(a0 + evac_done[m - 4]))
                        for k in range(32):
                            ins = t.matmul(
                                bank[:, 0:csz],
                                wv[:, k * 128:(k + 1) * 128],
                                ap("hfm", BF16, 128, csz, col0=k * TE + c0),
                                start=(k == 0), stop=(k == 31))
                        sPE.inc(ins)
                        m += 1

            @block.scalar
            def _(s):
                m = 0
                for i, ot in enumerate(tiles):
                    kind = kinds[i]
                    if i >= 1:
                        s.wait_ge(sDV.h, int(d0 + dv_cum[i]))
                    for ci, (c0, csz) in enumerate(TCHUNKS):
                        s.wait_ge(sPE.h, pe0 + m + 1)
                        ins = s.copy(ap("ip_u", F32, 128, csz, col0=c0),
                                     pbs[m % 4][:, 0:csz])
                        sAC.inc(ins)
                        m += 1
                    if kind == "conv":
                        s.wait_ge(sDV.h, int(d0 + dv_cum[i + 1]))
                        if i >= 2 and sp_per[i - 2]:
                            gSP.wait(s, spt[i - 2])
                        ins = s.activation(
                            ap(f"ip_c{i % 2}", BF16, 128, T),
                            ap("ip_acc", F32, 128, T), AF.Silu,
                            bias=ap("cb", F32, 128, 1, col0=ot - 64))
                        sAC.inc(ins)
                    elif kind == "dt":
                        s.wait_ge(sDV.h, int(d0 + dv_cum[i] + 1))
                        gSP.wait_all(s)
                        ins = s.activation(ap("ip_t2", F32, 128, T),
                                           ap("ip_t1", F32, 128, T), AF.Abs)
                        sAC.inc(ins)
                        s.wait_ge(sAC.h, sAC.n)
                        ins = s.activation(ap("ip_t2", F32, 128, T),
                                           ap("ip_t2", F32, 128, T), AF.Exp,
                                           scale=-1.0)
                        sAC.inc(ins)
                        s.wait_ge(sAC.h, sAC.n)
                        ins = s.activation(ap("ip_t2", F32, 128, T),
                                           ap("ip_t2", F32, 128, T), AF.Ln, bias=1.0)
                        sAC.inc(ins)

            @block.vector
            def _(v):
                for i, ot in enumerate(tiles):
                    kind = kinds[i]
                    v.wait_ge(sAC.h, int(a0 + act_cum[i] + 5))
                    if kind == "conv":
                        accv = ap("ip_acc", F32, 128, T)

                        def uv(o0):
                            return ap("ip_u", F32, 128, T, col0=o0)

                        def wc(tap):
                            return ap("cw", F32, 128, 1, col0=(ot - 64) * 4 + tap)
                        ins = v.tensor_scalar(accv, uv(13), wc(0), None, ALU.mult)
                        sDV.inc(ins)
                        v.wait_ge(sDV.h, sDV.n)
                        ins = v.scalar_tensor_tensor(accv, uv(14), wc(1), accv,
                                                     ALU.mult, ALU.add)
                        sDV.inc(ins)
                        v.wait_ge(sDV.h, sDV.n)
                        ins = v.scalar_tensor_tensor(accv, uv(15), wc(2), accv,
                                                     ALU.mult, ALU.add)
                        sDV.inc(ins)
                        v.wait_ge(sDV.h, sDV.n)
                        ins = v.scalar_tensor_tensor(accv, uv(16), wc(3), accv,
                                                     ALU.mult, ALU.add)
                        sDV.inc(ins)
                    elif kind == "gate":
                        if i >= 2 and sp_per[i - 2]:
                            gSP.wait(v, spt[i - 2])
                        ins = v.tensor_copy(ap(f"ip_c{i % 2}", BF16, 128, T),
                                            ap("ip_u", F32, 128, T, col0=PFX))
                        sDV.inc(ins)
                    else:
                        ins = v.tensor_scalar(ap("ip_t1", F32, 128, T),
                                              ap("ip_u", F32, 128, T, col0=PFX),
                                              ap("dtb", F32, 128, 1), None, ALU.add)
                        sDV.inc(ins)
                        v.wait_ge(sAC.h, int(a0 + act_cum[i] + 8))
                        ins = v.tensor_scalar(ap("ip_t1", F32, 128, T),
                                              ap("ip_t1", F32, 128, T),
                                              0.0, None, ALU.max)
                        sDV.inc(ins)
                        v.wait_ge(sDV.h, sDV.n)
                        ins = v.tensor_tensor(ap("dtfm", F32, 128, T),
                                              ap("ip_t1", F32, 128, T),
                                              ap("ip_t2", F32, 128, T), ALU.add)
                        sDV.inc(ins)
                        v.wait_ge(sDV.h, sDV.n)
                        ins = v.tensor_scalar(ap("dafm", F32, 128, T),
                                              ap("dtfm", F32, 128, T),
                                              ap("acol", F32, 128, 1), None, ALU.mult)
                        sDV.inc(ins)

        def emit_chain():
            @block.vector
            def _(v):
                barrier(v)
                ins = v.memset(ap("pst", F32, 128, DI), 0.0)
                sDV.inc(ins)

            pe0 = sPE.n
            d0 = sDV.n
            a0 = sAC.n
            ht = []
            bt = []

            PE_C = 19
            DV_C = 21

            @block.sync
            def _(sy):
                barrier(sy)
                for c in range(NCH):
                    if c >= 1:
                        sy.wait_ge(sDV.h, d0 + DV_C * (c - 1) + 4)
                    ht.append(gTD.dma(sy, out=ap("sc_hid", BF16, 128, DI),
                                      in_=hid_scr[:, c * CH:(c + 1) * CH],
                                      transpose=True))
                    if c >= 1:
                        sy.wait_ge(sPE.h, pe0 + PE_C * (c - 1) + 3)
                    bt.append(gX.dma(
                        sy, out=ap("sc_bfm", BF16, 128, 8 * 128),
                        in_=b_scr.rearrange("(g n) l -> n g l", n=128)[
                            :, :, c * CH:(c + 1) * CH]))

            @block.tensor
            def _(t):
                t.wait_ge(sDV.h, d0)
                t.wait_ge(sAC.h, a0)
                for c in range(NCH):
                    if c >= 1:
                        t.wait_ge(sAC.h, a0 + 2 * c)
                        t.wait_ge(sDV.h, d0 + DV_C * (c - 1) + 1)
                    t.transpose(pbs[6][:, 0:128],
                                ap("dafm", F32, 128, 128, col0=c * CH),
                                ap("idf", F32, 128, 128))
                    ins = t.transpose(pbs[7][:, 0:128],
                                      ap("dtfm", F32, 128, 128, col0=c * CH),
                                      ap("idf", F32, 128, 128))
                    sPE.inc(ins)
                    t.wait_ge(sDV.h, d0 + DV_C * c + 1)
                    t.matmul(pbs[6][:, 0:128], ap("csu", F32, 128, 128),
                             ap("sc_datm", F32, 128, 128), start=True, stop=True)
                    ins = t.matmul(pbs[7][:, 0:128], ap("c1s", F32, 128, 128),
                                   ap("sc_datm", F32, 128, 128),
                                   start=True, stop=True)
                    sPE.inc(ins)
                    gX.wait(t, bt[c])
                    if c >= 1:
                        t.wait_ge(sDV.h, d0 + DV_C * (c - 1) + 2)
                    for gi in range(8):
                        bank = pbs[0].ap().bitcast(BF16)
                        ins = t.transpose(bank[:, gi * 128:(gi + 1) * 128],
                                          ap("sc_bfm", BF16, 128, 128,
                                             col0=gi * 128),
                                          ap("ibf", BF16, 128, 128))
                    sPE.inc(ins)
                    t.wait_ge(sDV.h, d0 + DV_C * c + 4)
                    for f in range(16):
                        bank = pbs[2 + f % 2]
                        if f >= 2:
                            t.wait_ge(sDV.h, d0 + DV_C * c + 4 + f)
                        elif c >= 1:
                            t.wait_ge(sDV.h, d0 + DV_C * (c - 1) + 20 + f)
                        for hh in range(8):
                            hd = f * 8 + hh
                            ins = t.matmul(
                                bank[:, hh * 64:(hh + 1) * 64],
                                ap("sc_btm", BF16, 128, 128,
                                   col0=(hd // 16) * 128),
                                ap("sc_hd2big", BF16, 128, 64, col0=hd * 64),
                                start=True, stop=True)
                        sPE.inc(ins)

            @block.scalar
            def _(s):
                for c in range(NCH):
                    if c >= 1:
                        s.wait_ge(sDV.h, d0 + DV_C * (c - 1) + 5)
                    s.wait_ge(sPE.h, pe0 + PE_C * c + 2)
                    ins = s.activation(ap("sc_dec", F32, 128, 128), pbs[6][:, 0:128],
                                       AF.Exp)
                    sAC.inc(ins)
                    ins = s.activation(ap("sc_etb", F32, 128, 128), pbs[7][:, 0:128],
                                       AF.Exp)
                    sAC.inc(ins)

            @block.vector
            def _(v):
                for c in range(NCH):
                    v.wait_ge(sPE.h, pe0 + PE_C * c + 1)
                    v.tensor_copy(ap("sc_datm", F32, 128, 128), pbs[6][:, 0:128])
                    ins = v.tensor_copy(ap("sc_dttm", F32, 128, 128),
                                        pbs[7][:, 0:128])
                    sDV.inc(ins)
                    v.wait_ge(sPE.h, pe0 + PE_C * c + 3)
                    ins = v.tensor_copy(ap("sc_btm", BF16, 128, 1024),
                                        pbs[0].ap().bitcast(BF16)[:, 0:1024])
                    sDV.inc(ins)
                    v.wait_ge(sAC.h, a0 + 2 * c + 1)
                    ins = v.tensor_tensor(ap("sc_w2", F32, 128, 128),
                                          ap("sc_dttm", F32, 128, 128),
                                          ap("sc_dec", F32, 128, 128), ALU.mult)
                    sDV.inc(ins)
                    v.wait_ge(sDV.h, d0 + DV_C * c + 3)
                    gTD.wait(v, ht[c])
                    ins = v.tensor_tensor(h3(ap("sc_hd2big", BF16, 128, DI)),
                                          h3(ap("sc_hid", BF16, 128, DI)),
                                          w3(ap("sc_w2", F32, 128, 128)),
                                          ALU.mult)
                    sDV.inc(ins)
                    v.wait_ge(sAC.h, a0 + 2 * c + 2)
                    if c >= 1:
                        v.wait_ge(sDV.h, d0 + DV_C * (c - 1) + 21)
                    else:
                        v.wait_ge(sDV.h, d0)
                    ins = v.tensor_tensor(h3(ap("pst", F32, 128, DI)),
                                          h3(ap("pst", F32, 128, DI)),
                                          w3(ap("sc_etb", F32, 128, 128)),
                                          ALU.mult)
                    sDV.inc(ins)
                    v.wait_ge(sDV.h, d0 + DV_C * c + 5)
                    for f in range(16):
                        bank = pbs[2 + f % 2]
                        v.wait_ge(sPE.h, pe0 + PE_C * c + 4 + f)
                        ins = v.tensor_tensor(
                            ap("pst", F32, 128, 512, col0=f * 512),
                            ap("pst", F32, 128, 512, col0=f * 512),
                            bank[:, 0:512], ALU.add)
                        sDV.inc(ins)

            @block.vector
            def _(v):
                v.wait_ge(sDV.h, d0 + DV_C * NCH)
                ins = v.tensor_scalar(ap("pst", F32, 128, DI), ap("pst", F32, 128, DI),
                                      ap("mskc", F32, 128, 1), None, ALU.mult)
                sDV.inc(ins)

            @block.gpsimd
            def _(g):
                g.wait_ge(sDV.h, sDV.n)
                gSP.dma(g, out=p_scr, in_=ap("pst", F32, 128, DI))

        def emit_scan():
            pe0 = sPE.n
            d0 = sDV.n
            a0 = sAC.n
            PE_C2, DV_C2, AC_C2 = 77, 160, 68

            def PEc(c, k):
                return int(pe0 + PE_C2 * c + k)

            def DVc(c, k):
                return int(d0 + DV_C2 * c + k)

            def ACc(c, k):
                return int(a0 + AC_C2 * c + k)

            def cc(i):
                return ap("col", F32, 128, 1, col0=4 * i)

            ht = []
            gt = []
            bt = []
            ct = []
            t_pst = []
            tok_acs = [None] * NCH
            tok_row = [[None] * 8 for _ in range(NCH)]
            tok_y2 = [None] * NCH

            @block.sync
            def _(sy):
                barrier(sy)
                t_pst.append(gX.dma(sy, out=ap("pst", F32, 128, DI), in_=p_scr))
                for c in range(NCH):
                    csl = slice(c * CH, (c + 1) * CH)
                    if c >= 1:
                        sy.wait_ge(sDV.h, DVc(c - 1, 143))
                    ht.append(gTD.dma(sy, out=ap("sc_hid", BF16, 128, DI),
                                      in_=hid_scr[:, csl], transpose=True))
                    if c >= 1:
                        sy.wait_ge(sAC.h, ACc(c - 1, 66))
                    gt.append(gTD.dma(sy, out=ap("sc_gate", BF16, 128, DI),
                                      in_=gate_scr[:, csl], transpose=True))
                    if c >= 1:
                        sy.wait_ge(sPE.h, PEc(c - 1, 55))
                    bt.append(gX.dma(
                        sy, out=ap("sc_bfm", BF16, 128, 8 * 128),
                        in_=b_scr.rearrange("(g n) l -> n g l", n=128)[:, :, csl]))
                    if c >= 1:
                        sy.wait_ge(sPE.h, PEc(c - 1, 61))
                    ct.append(gX.dma(
                        sy, out=ap("sc_cfm", BF16, 128, 8 * 128),
                        in_=c_scr.rearrange("(g n) l -> n g l", n=128)[:, :, csl]))

            @block.gpsimd
            def _(g_):
                for c in range(NCH):
                    g_.wait_ge(sDV.h, DVc(c, 3))
                    tok_acs[c] = gSP.dma(g_, out=acs_scr,
                                         in_=ap("sc_acsf", F32, 128, 128))
                    for g in range(8):
                        if g == 0:
                            gSP.wait(g_, tok_acs[c])
                            if c >= 1:
                                g_.wait_ge(sPE.h, PEc(c - 1, 59))
                        else:
                            g_.wait_ge(sPE.h, PEc(c, 7 * g + 3))
                        tok_row[c][g] = gSP.dma(
                            g_, out=ap("sc_row", F32, 1, 2048),
                            in_=acs_scr[16 * g:16 * (g + 1), :])
                    g_.wait_ge(sDV.h, DVc(c, 141))
                    tok_y2[c] = gSP.dma(g_, out=y2_scr[c * CH:(c + 1) * CH, :],
                                        in_=ap("sc_y2", BF16, 128, DI))

            @block.tensor
            def _(t):
                for c in range(NCH):
                    if c == 0:
                        t.wait_ge(sDV.h, d0)
                        t.wait_ge(sAC.h, a0)
                    else:
                        t.wait_ge(sDV.h, DVc(c - 1, 160))
                        t.wait_ge(sAC.h, ACc(c - 1, 3))
                    t.transpose(pbs[6][:, 0:128],
                                ap("dafm", F32, 128, 128, col0=c * CH),
                                ap("idf", F32, 128, 128))
                    ins = t.transpose(pbs[7][:, 0:128],
                                      ap("dtfm", F32, 128, 128, col0=c * CH),
                                      ap("idf", F32, 128, 128))
                    sPE.inc(ins)
                    t.wait_ge(sDV.h, DVc(c, 1))
                    t.matmul(pbs[6][:, 0:128], ap("cut", F32, 128, 128),
                             ap("sc_datm", F32, 128, 128), start=True, stop=True)
                    ins = t.matmul(pbs[7][:, 0:128], ap("csu", F32, 128, 128),
                                   ap("sc_datm", F32, 128, 128),
                                   start=True, stop=True)
                    sPE.inc(ins)
                    t.wait_ge(sDV.h, DVc(c, 2))
                    ins = t.matmul(pbs[6][:, 0:128], ap("c1s", F32, 128, 128),
                                   ap("sc_datm", F32, 128, 128),
                                   start=True, stop=True)
                    sPE.inc(ins)
                    t.wait_ge(sAC.h, ACc(c, 2))
                    ins = t.transpose(pbs[7][:, 0:128],
                                      ap("sc_acs", F32, 128, 128),
                                      ap("idf", F32, 128, 128))
                    sPE.inc(ins)
                    gX.wait(t, bt[c])
                    for gi in range(8):
                        bank = pbs[0].ap().bitcast(BF16)
                        ins = t.transpose(bank[:, gi * 128:(gi + 1) * 128],
                                          ap("sc_bfm", BF16, 128, 128,
                                             col0=gi * 128),
                                          ap("ibf", BF16, 128, 128))
                    sPE.inc(ins)
                    gX.wait(t, ct[c])
                    for g in range(8):
                        if g == 0:
                            t.wait_ge(sDV.h, DVc(c, 3))
                        else:
                            t.wait_ge(sDV.h, DVc(c, 6 + 17 * (g - 1)))
                        ins = t.matmul(pbs[7][:, 0:128],
                                       ap("sc_bfm", BF16, 128, 128, col0=g * 128),
                                       ap("sc_cfm", BF16, 128, 128, col0=g * 128),
                                       start=True, stop=True)
                        sPE.inc(ins)
                        for si in range(2):
                            h0 = 16 * g + 8 * si
                            gSP.wait(t, tok_row[c][g])
                            if not (g == 0 and si == 0):
                                prev_stt = (DVc(c, 7 + 17 * g) if si == 1
                                            else DVc(c, 14 + 17 * (g - 1)))
                                t.wait_ge(sDV.h, prev_stt)
                            for q in range(2):
                                ins = t.matmul(
                                    pbs[1 + 4 * q][:, 0:512],
                                    ap("c1r", F32, 1, 128),
                                    ap("sc_row", F32, 1, 512,
                                       col0=(8 * si + 4 * q) * 128),
                                    start=True, stop=True)
                            sPE.inc(ins)
                            t.wait_ge(sDV.h, DVc(c, 9 + 17 * g + 7 * si))
                            if not (g == 0 and si == 0):
                                prev_y8 = (DVc(c, 11 + 17 * g) if si == 1
                                           else DVc(c, 18 + 17 * (g - 1)))
                                t.wait_ge(sDV.h, prev_y8)
                            t.matmul(pbs[2][:, 0:512],
                                     ap("ibf", BF16, 128, 128),
                                     ap("sc_dph", BF16, 128, 512),
                                     start=True, stop=False)
                            for hp in range(8):
                                ins = t.matmul(pbs[2][:, hp * 64:(hp + 1) * 64],
                                               ap("sc_mt", BF16, 128, 128,
                                                  col0=hp * 128),
                                               ap("sc_hdbig", BF16, 128, 64,
                                                  col0=(h0 + hp) * 64),
                                               start=False, stop=(hp == 7))
                            sPE.inc(ins)
                            if not (g == 0 and si == 0):
                                prev_t1 = (DVc(c, 10 + 17 * g) if si == 1
                                           else DVc(c, 17 + 17 * (g - 1)))
                                t.wait_ge(sDV.h, prev_t1)
                            ins = t.matmul(pbs[3][:, 0:512],
                                           ap("sc_cfm", BF16, 128, 128,
                                              col0=g * 128),
                                           ap("sc_pbf", BF16, 128, 512,
                                              col0=h0 * 64),
                                           start=True, stop=True)
                            sPE.inc(ins)
                    for f in range(16):
                        bank = pbs[4 + 2 * (f % 2)]
                        if f == 0:
                            t.wait_ge(sDV.h, DVc(c, 143))
                        elif f == 1:
                            t.wait_ge(sAC.h, ACc(c, 3))
                        else:
                            t.wait_ge(sDV.h, DVc(c, 143 + f))
                        for hh in range(8):
                            hd_ = f * 8 + hh
                            ins = t.matmul(
                                bank[:, hh * 64:(hh + 1) * 64],
                                ap("sc_btm", BF16, 128, 128,
                                   col0=(hd_ // 16) * 128),
                                ap("sc_hd2big", BF16, 128, 64, col0=hd_ * 64),
                                start=True, stop=True)
                        sPE.inc(ins)

            @block.scalar
            def _(s):
                for c in range(NCH):
                    if c == 0:
                        gX.wait(s, t_pst[0])
                        s.wait_ge(sDV.h, d0)
                    else:
                        s.wait_ge(sDV.h, DVc(c - 1, 160))
                        s.wait_ge(sPE.h, PEc(c - 1, 61))
                    ins = s.copy(ap("sc_pbf", BF16, 128, DI),
                                 ap("pst", F32, 128, DI))
                    sAC.inc(ins)
                    s.wait_ge(sPE.h, PEc(c, 2))
                    ins = s.activation(ap("sc_dec", F32, 128, 128),
                                       pbs[7][:, 0:128], AF.Exp)
                    sAC.inc(ins)
                    s.wait_ge(sPE.h, PEc(c, 3))
                    ins = s.activation(ap("sc_etb", F32, 128, 128),
                                       pbs[6][:, 0:128], AF.Exp)
                    sAC.inc(ins)
                    s.wait_ge(sDV.h, DVc(c, 2))
                    ins = s.activation(ap("sc_eacs", F32, 128, 128),
                                       ap("sc_acs", F32, 128, 128), AF.Exp)
                    sAC.inc(ins)
                    for g in range(8):
                        for si in range(2):
                            h0 = 16 * g + 8 * si
                            s.wait_ge(sDV.h, DVc(c, 7 + 17 * g + 7 * si))
                            ins = s.activation(ap("sc_dsb", F32, 128, 1024),
                                               ap("sc_dsb", F32, 128, 1024),
                                               AF.Relu, scale=-1.0)
                            sAC.inc(ins)
                            s.wait_ge(sAC.h, sAC.n)
                            ins = s.activation(ap("sc_dsb", F32, 128, 1024),
                                               ap("sc_dsb", F32, 128, 1024),
                                               AF.Exp, scale=-1.0)
                            sAC.inc(ins)
                            if g == 0 and si == 0:
                                gTD.wait(s, gt[c])
                            else:
                                prev_yg = (DVc(c, 12 + 17 * g) if si == 1
                                           else DVc(c, 19 + 17 * (g - 1)))
                                s.wait_ge(sDV.h, prev_yg)
                            ins = s.activation(ap("sc_sg", F32, 128, 512),
                                               ap("sc_gate", BF16, 128, 512,
                                                  col0=h0 * 64), AF.Silu)
                            sAC.inc(ins)
                        s.wait_ge(sDV.h, DVc(c, 21 + 17 * g))
                        ins = s.activation(cc(13), cc(12), AF.Ln,
                                           bias=ap("epsc", F32, 128, 1),
                                           scale=1.0 / 1024.0)
                        sAC.inc(ins)
                        s.wait_ge(sAC.h, sAC.n)
                        ins = s.activation(cc(13), cc(13), AF.Exp, scale=-0.5)
                        sAC.inc(ins)

            @block.vector
            def _(v):
                for c in range(NCH):
                    v.wait_ge(sPE.h, PEc(c, 1))
                    v.tensor_copy(ap("sc_datm", F32, 128, 128), pbs[6][:, 0:128])
                    ins = v.tensor_copy(ap("sc_dttm", F32, 128, 128),
                                        pbs[7][:, 0:128])
                    sDV.inc(ins)
                    v.wait_ge(sPE.h, PEc(c, 2))
                    if c >= 1:
                        v.wait_ge(sAC.h, ACc(c - 1, 4))
                    ins = v.tensor_copy(ap("sc_acs", F32, 128, 128),
                                        pbs[6][:, 0:128])
                    sDV.inc(ins)
                    v.wait_ge(sPE.h, PEc(c, 4))
                    if c >= 1:
                        gSP.wait(v, tok_acs[c - 1])
                    ins = v.tensor_copy(ap("sc_acsf", F32, 128, 128),
                                        pbs[7][:, 0:128])
                    sDV.inc(ins)
                    v.wait_ge(sPE.h, PEc(c, 5))
                    ins = v.tensor_copy(ap("sc_btm", BF16, 128, 1024),
                                        pbs[0].ap().bitcast(BF16)[:, 0:1024])
                    sDV.inc(ins)
                    gTD.wait(v, ht[c])
                    v.wait_ge(sDV.h, DVc(c, 1))
                    ins = v.tensor_tensor(h3(ap("sc_hdbig", BF16, 128, DI)),
                                          h3(ap("sc_hid", BF16, 128, DI)),
                                          w3(ap("sc_dttm", F32, 128, 128)),
                                          ALU.mult)
                    sDV.inc(ins)
                    for g in range(8):
                        v.wait_ge(sPE.h, PEc(c, 6 + 7 * g))
                        ins = v.tensor_tensor(ap("sc_gm", F32, 128, 128),
                                              pbs[7][:, 0:128],
                                              ap("ctr", F32, 128, 128), ALU.mult)
                        sDV.inc(ins)
                        for si in range(2):
                            h0 = 16 * g + 8 * si
                            v.wait_ge(sPE.h, PEc(c, 7 + 7 * g + 3 * si))
                            for q in range(2):
                                ins = v.scalar_tensor_tensor(
                                    ap("sc_dsb", F32, 128, 512,
                                       col0=512 * q).rearrange(
                                           "p (h i) -> p h i", i=128),
                                    pbs[1 + 4 * q][:, 0:512].rearrange(
                                        "p (h i) -> p h i", i=128),
                                    1.0,
                                    ap("sc_acs", F32, 128, 4,
                                       col0=h0 + 4 * q).rearrange(
                                           "p (h o) -> p h o",
                                           o=1).broadcast_to([128, 4, 128]),
                                    ALU.mult, ALU.subtract)
                            sDV.inc(ins)
                            v.wait_ge(sAC.h, ACc(c, 6 + 8 * g + 3 * si))
                            ins = v.scalar_tensor_tensor(
                                ap("sc_mt", BF16, 128, 1024).rearrange(
                                    "p (h i) -> p h i", i=128),
                                ap("sc_dsb", F32, 128, 1024).rearrange(
                                    "p (h i) -> p h i", i=128),
                                1.0,
                                ap("sc_gm", F32, 128, 128).rearrange(
                                    "p (a i) -> p a i",
                                    a=1).broadcast_to([128, 8, 128]),
                                ALU.min, ALU.mult)
                            sDV.inc(ins)
                            if not (g == 0 and si == 0):
                                prev_yd = (PEc(c, 8 + 7 * g) if si == 1
                                           else PEc(c, 4 + 7 * g))
                                v.wait_ge(sPE.h, prev_yd)
                            ins = v.tensor_tensor(
                                h3(ap("sc_dph", BF16, 128, 512)),
                                h3(ap("sc_hid", BF16, 128, 512, col0=h0 * 64)),
                                w3(ap("dpbc", F32, 128, 8, col0=h0), 8),
                                ALU.mult)
                            sDV.inc(ins)
                            v.wait_ge(sPE.h, PEc(c, 9 + 7 * g + 3 * si))
                            ins = v.tensor_tensor(
                                h3(ap("sc_t1", F32, 128, 512)),
                                pbs[3][:, 0:512].rearrange("p (h o) -> p h o",
                                                           o=64),
                                w3(ap("sc_eacs", F32, 128, 8, col0=h0), 8),
                                ALU.mult)
                            sDV.inc(ins)
                            v.wait_ge(sDV.h, DVc(c, 10 + 17 * g + 7 * si))
                            v.wait_ge(sPE.h, PEc(c, 8 + 7 * g + 3 * si))
                            ins = v.tensor_tensor(ap("sc_t1", F32, 128, 512),
                                                  pbs[2][:, 0:512],
                                                  ap("sc_t1", F32, 128, 512),
                                                  ALU.add)
                            sDV.inc(ins)
                            v.wait_ge(sAC.h, ACc(c, 7 + 8 * g + 3 * si))
                            v.wait_ge(sDV.h, DVc(c, 11 + 17 * g + 7 * si))
                            ins = v.tensor_tensor(
                                ap("sc_yg", F32, 128, 512, col0=512 * si),
                                ap("sc_t1", F32, 128, 512),
                                ap("sc_sg", F32, 128, 512), ALU.mult)
                            sDV.inc(ins)
                            v.wait_ge(sDV.h, DVc(c, 12 + 17 * g + 7 * si))
                            ins = v.scalar_tensor_tensor(
                                ap("sc_sqj", F32, 128, 512),
                                ap("sc_yg", F32, 128, 512, col0=512 * si), 1.0,
                                ap("sc_yg", F32, 128, 512, col0=512 * si),
                                ALU.mult, ALU.mult, accum_out=cc(10 + si))
                            sDV.inc(ins)
                        v.wait_ge(sDV.h, DVc(c, 20 + 17 * g))
                        ins = v.tensor_tensor(cc(12), cc(10), cc(11), ALU.add)
                        sDV.inc(ins)
                        v.wait_ge(sAC.h, ACc(c, 12 + 8 * g))
                        if g == 0 and c >= 1:
                            gSP.wait(v, tok_y2[c - 1])
                        ins = v.tensor_scalar(
                            ap("sc_y2", BF16, 128, 1024, col0=1024 * g),
                            ap("sc_yg", F32, 128, 1024), cc(13), None, ALU.mult)
                        sDV.inc(ins)
                    v.wait_ge(sAC.h, ACc(c, 2))
                    ins = v.tensor_tensor(ap("sc_w2", F32, 128, 128),
                                          ap("sc_dttm", F32, 128, 128),
                                          ap("sc_dec", F32, 128, 128), ALU.mult)
                    sDV.inc(ins)
                    v.wait_ge(sDV.h, DVc(c, 142))
                    ins = v.tensor_tensor(h3(ap("sc_hd2big", BF16, 128, DI)),
                                          h3(ap("sc_hid", BF16, 128, DI)),
                                          w3(ap("sc_w2", F32, 128, 128)),
                                          ALU.mult)
                    sDV.inc(ins)
                    v.wait_ge(sAC.h, ACc(c, 3))
                    ins = v.tensor_tensor(h3(ap("pst", F32, 128, DI)),
                                          h3(ap("pst", F32, 128, DI)),
                                          w3(ap("sc_etb", F32, 128, 128)),
                                          ALU.mult)
                    sDV.inc(ins)
                    v.wait_ge(sDV.h, DVc(c, 144))
                    for f in range(16):
                        bank = pbs[4 + 2 * (f % 2)]
                        v.wait_ge(sPE.h, PEc(c, 62 + f))
                        ins = v.tensor_tensor(
                            ap("pst", F32, 128, 512, col0=f * 512),
                            ap("pst", F32, 128, 512, col0=f * 512),
                            bank[:, 0:512], ALU.add)
                        sDV.inc(ins)

        def emit_outproj():
            snap = snapshot()
            pe0 = sPE.n
            d0 = sDV.n
            wt = []
            yt = [[[None] * 64 for _ in range(2)] for _ in range(8)]
            spt = [[[None] * 8 for _ in range(2)] for _ in range(8)]

            def dv_fold(oc):
                return int(d0 + 17 * oc + 1)

            def dv_evac(oc, tb, j):
                return int(d0 + 17 * oc + 1 + 8 * tb + j + 1)

            def pe_mm(oc, tb, kt):
                return int(pe0 + 128 * oc + 64 * tb + kt + 1)

            @block.gpsimd
            def _(g_):
                wait_snap(g_, snap)
                for oc in range(8):
                    if oc >= 1:
                        g_.wait_ge(sPE.h, int(pe0 + 128 * oc))
                    wt.append(gW.dma(
                        g_, out=ap("op_w", BF16, 128, 64 * 512),
                        in_=wopT[:, oc * 512:(oc + 1) * 512].rearrange(
                            "(kt p) o -> p kt o", p=128)))
                    for tb in range(2):
                        for j in range(8):
                            g_.wait_ge(sDV.h, dv_evac(oc, tb, j))
                            spt[oc][tb][j] = gSP.dma(
                                g_, out=mo_scr[tb * 1024 + j * 128:
                                               tb * 1024 + (j + 1) * 128,
                                               oc * 512:(oc + 1) * 512],
                                in_=ap(f"op_s{j % 2}", F32, 128, 512))

            @block.sync
            def _(sy):
                wait_snap(sy, snap)
                for oc in range(8):
                    for tb in range(2):
                        for kt in range(64):
                            if not (oc == 0 and tb == 0 and kt < 2):
                                prev = 128 * oc + 64 * tb + kt - 2
                                sy.wait_ge(sPE.h, int(pe0 + prev + 1))
                            yt[oc][tb][kt] = gTD.dma(
                                sy, out=ap(f"op_y{'ab'[kt % 2]}", BF16, 128,
                                           1024),
                                in_=y2_scr[tb * 1024:(tb + 1) * 1024,
                                           kt * 128:(kt + 1) * 128],
                                transpose=True)

            @block.tensor
            def _(t):
                for oc in range(8):
                    for tb in range(2):
                        for kt in range(64):
                            gTD.wait(t, yt[oc][tb][kt])
                            if kt == 0:
                                t.wait_ge(sDV.h, dv_fold(oc))
                                if oc or tb:
                                    pb_ = (oc * 2 + tb) - 1
                                    t.wait_ge(sDV.h, dv_evac(pb_ // 2,
                                                             pb_ % 2, 7))
                            for tt in range(8):
                                ins = t.matmul(
                                    pbs[tt][:, 0:512],
                                    ap(f"op_y{'ab'[kt % 2]}", BF16, 128, 128,
                                       col0=tt * 128),
                                    ap("op_w", BF16, 128, 512, col0=kt * 512),
                                    start=(kt == 0), stop=(kt == 63))
                            sPE.inc(ins)

            @block.vector
            def _(v):
                for oc in range(8):
                    gW.wait(v, wt[oc])
                    ins = v.tensor_tensor(
                        ap("op_w", BF16, 128, 64 * 512).rearrange(
                            "p (k o) -> p k o", o=512),
                        ap("op_w", BF16, 128, 64 * 512).rearrange(
                            "p (k o) -> p k o", o=512),
                        ap("gnw", F32, 128, 64).rearrange(
                            "p (h o) -> p h o", o=1).broadcast_to(
                                [128, 64, 512]),
                        ALU.mult)
                    sDV.inc(ins)
                    for tb in range(2):
                        for j in range(8):
                            v.wait_ge(sPE.h, pe_mm(oc, tb, 63))
                            e = oc * 16 + tb * 8 + j
                            if e >= 2:
                                pj = e - 2
                                gSP.wait(v, spt[pj // 16][(pj % 16) // 8]
                                         [pj % 8])
                            ins = v.tensor_copy(ap(f"op_s{j % 2}", F32, 128,
                                                   512), pbs[j][:, 0:512])
                            sDV.inc(ins)

        def emit_epilogue():
            d0 = sDV.n
            a0 = sAC.n
            xts = []
            mts = []
            uts = [None] * 16
            vts = [None] * 16

            def cc(i, rr=128):
                return ap("col", F32, rr, 1, col0=4 * i)

            @block.sync
            def _(sy):
                barrier(sy)
                for i in range(16):
                    if i >= 2:
                        sy.wait_ge(sDV.h, int(d0 + 17 * (i - 2) + 12))
                    xts.append(gX.dma(sy, out=ap(f"me_x{i % 2}", F32, 128, DM),
                                      in_=x_own[PFX + i * 128:
                                                PFX + (i + 1) * 128, :]))
                    if i >= 2:
                        sy.wait_ge(sDV.h, int(d0 + 17 * (i - 2) + 11))
                    mts.append(gX.dma(sy, out=ap(f"me_mo{i % 2}", F32, 128,
                                                 DM),
                                      in_=mo_scr[i * 128:(i + 1) * 128, :]))

            @block.gpsimd
            def _(g_):
                for i in range(16):
                    g_.wait_ge(sDV.h, int(d0 + 17 * i + 13))
                    uts[i] = gSP.dma(g_, out=u_scr[i * 128:(i + 1) * 128, :],
                                     in_=ap("me_u", F32, 128, DM))
                    g_.wait_ge(sDV.h, int(d0 + 17 * i + 17))
                    vts[i] = gSP.dma(g_, out=v_scr[i * 128:(i + 1) * 128, :],
                                     in_=ap("me_v", BF16, 128, DM))

            @block.scalar
            def _(s):
                for i in range(16):
                    DP = d0 + 17 * i
                    for ln in range(3):
                        src = [ap(f"me_x{i % 2}", F32, 128, DM),
                               ap(f"me_mo{i % 2}", F32, 128, DM),
                               ap("me_u", F32, 128, DM)][ln]
                        s.wait_ge(sDV.h, int(DP + [1, 7, 13][ln]))
                        ins = s.activation(ap("me_j", BF16, 128, DM), src,
                                           AF.Square, accum_out=cc(1))
                        sAC.inc(ins)
                        s.wait_ge(sDV.h, int(DP + [3, 9, 15][ln]))
                        ins = s.activation(cc(5), cc(4), AF.Ln,
                                           bias=ap("epsc", F32, 128, 1))
                        sAC.inc(ins)
                        s.wait_ge(sAC.h, sAC.n)
                        ins = s.activation(cc(5), cc(5), AF.Exp, scale=-0.5)
                        sAC.inc(ins)

            @block.vector
            def _(v):
                for i in range(16):
                    DP = d0 + 17 * i
                    AP_ = a0 + 9 * i
                    xv = ap(f"me_x{i % 2}", F32, 128, DM)
                    mv = ap(f"me_mo{i % 2}", F32, 128, DM)
                    uv = ap("me_u", F32, 128, DM)
                    gX.wait(v, xts[i])
                    for ln in range(3):
                        src = [xv, mv, uv][ln]
                        if ln == 1:
                            gX.wait(v, mts[i])
                            v.wait_ge(sDV.h, int(DP + 6))
                        if ln == 2:
                            v.wait_ge(sDV.h, int(DP + 12))
                        ins = v.tensor_scalar(src, src, 1.0, None, ALU.mult,
                                              ALU.add, accum_out=cc(0))
                        sDV.inc(ins)
                        v.wait_ge(sAC.h, int(AP_ + 3 * ln + 1))
                        v.tensor_tensor(cc(3), cc(0), cc(0), ALU.mult)
                        ins = v.tensor_scalar(cc(2), cc(1), 1.0 / DM, None,
                                              ALU.mult)
                        sDV.inc(ins)
                        v.wait_ge(sDV.h, sDV.n)
                        ins = v.scalar_tensor_tensor(cc(4), cc(3),
                                                     -1.0 / (DM * DM), cc(2),
                                                     ALU.mult, ALU.add)
                        sDV.inc(ins)
                        v.wait_ge(sAC.h, int(AP_ + 3 * ln + 3))
                        ins = v.tensor_scalar(cc(6), cc(5), cc(0), 1.0 / DM,
                                              ALU.mult, ALU.mult)
                        sDV.inc(ins)
                        v.wait_ge(sDV.h, sDV.n)
                        if ln == 2:
                            if i >= 1:
                                gSP.wait(v, vts[i - 1])
                            ins = v.tensor_scalar(ap("me_v", BF16, 128, DM),
                                                  uv, cc(5), cc(6),
                                                  ALU.mult, ALU.subtract)
                            sDV.inc(ins)
                        else:
                            if ln == 0 and i >= 1:
                                gSP.wait(v, uts[i - 1])
                            ins = v.tensor_scalar(uv, src, cc(5), cc(6),
                                                  ALU.mult, ALU.subtract)
                            sDV.inc(ins)
                            v.wait_ge(sDV.h, sDV.n)
                            if ln == 0:
                                gX.wait(v, mts[i])
                                ins = v.tensor_tensor(mv, mv, uv, ALU.add)
                            else:
                                ins = v.tensor_tensor(uv, uv, xv, ALU.add)
                            sDV.inc(ins)

        def emit_ffn1():
            pe0 = sPE.n
            d0 = sDV.n
            specs = []
            for j in range(31):
                specs.append((vhT[:, j * 128:(j + 1) * 128], 128, "stail", j,
                              mid_scr[j * 128:(j + 1) * 128, :]))
            specs.append((vhT[:, 3968:4032], 64, "stail", 31,
                          mid_scr[3968:4032, :]))
            specs.append((vhtopT, 64, "shead", 0, mid_scr[4032:4096, :]))
            specs.append((vhrT, 4, "sdc", 0, mdl_scr))
            nt = len(specs)
            vt = []
            wt = []
            spt = [None] * nt

            @block.sync
            def _(sy):
                barrier(sy)
                for kt in range(32):
                    vt.append(gTD.dma(
                        sy, out=ap("vfm", BF16, 128, T, col0=kt * T),
                        in_=v_scr[:, kt * 128:(kt + 1) * 128], transpose=True))

            @block.gpsimd
            def _(g_):
                gTD.wait_all(g_)
                for j, (wsrc, rr, snm, scol, dst) in enumerate(specs):
                    if j >= 2:
                        g_.wait_ge(sPE.h, int(pe0 + 4 * (j - 1)))
                    wt.append(gW.dma(
                        g_, out=ap(f"f1_w{j % 2}", BF16, 128, 32 * rr),
                        in_=wsrc.rearrange("(kt p) r -> p kt r", p=128)))
                    if j >= 1:
                        pj = j - 1
                        g_.wait_ge(sDV.h, int(d0 + 4 * pj + 4))
                        rrp = specs[pj][1]
                        spt[pj] = gSP.dma(g_, out=specs[pj][4],
                                          in_=ap(f"f1_md{pj % 2}", BF16, rrp,
                                                 T))
                j = nt - 1
                g_.wait_ge(sDV.h, int(d0 + 4 * j + 4))
                spt[j] = gSP.dma(g_, out=specs[j][4],
                                 in_=ap(f"f1_md{j % 2}", BF16, specs[j][1], T))

            @block.tensor
            def _(t):
                gTD.wait_all(t)
                m = 0
                for j, (wsrc, rr, snm, scol, dst) in enumerate(specs):
                    gW.wait(t, wt[j])
                    for tc in range(4):
                        bank = pbs[m % 4]
                        if m >= 4:
                            t.wait_ge(sDV.h, int(d0 + m - 3))
                        for kt in range(32):
                            ins = t.matmul(
                                bank[0:rr, 0:512],
                                ap(f"f1_w{j % 2}", BF16, 128, rr,
                                   col0=kt * rr),
                                ap("vfm", BF16, 128, 512,
                                   col0=kt * T + tc * 512),
                                start=(kt == 0), stop=(kt == 31))
                        sPE.inc(ins)
                        m += 1

            @block.vector
            def _(v):
                m = 0
                for j, (wsrc, rr, snm, scol, dst) in enumerate(specs):
                    for tc in range(4):
                        v.wait_ge(sPE.h, int(pe0 + m + 1))
                        if tc == 0 and j >= 2 and spt[j - 2] is not None:
                            gSP.wait(v, spt[j - 2])
                        ins = v.tensor_scalar(
                            ap(f"f1_md{j % 2}", BF16, rr, 512,
                               col0=tc * 512),
                            pbs[m % 4][0:rr, 0:512],
                            ap(snm, F32, rr, 1, col0=scol), None, ALU.mult)
                        sDV.inc(ins)
                        m += 1

        def emit_ffn2():
            snap = snapshot()
            pe0 = sPE.n
            d0 = sDV.n
            mt = []
            bt = [None] * 8
            wts = [[None] * 3 for _ in range(8)]
            spt = [[None] * 16 for _ in range(8)]

            @block.gpsimd
            def _(g_):
                wait_snap(g_, snap)
                for oc in range(8):
                    if oc >= 1:
                        g_.wait_ge(sPE.h, int(pe0 + 16 * oc))
                    wts[oc][0] = gW.dma(
                        g_, out=ap("f2_ut", BF16, 128, 31 * 512),
                        in_=utT[0:3968, oc * 512:(oc + 1) * 512].rearrange(
                            "(kt p) o -> p kt o", p=128))
                    wts[oc][1] = gW.dma(
                        g_, out=ap("f2_ut", BF16, 64, 512, col0=31 * 512),
                        in_=utT[3968:4032, oc * 512:(oc + 1) * 512])
                    wts[oc][2] = gW.dma(
                        g_, out=ap("f2_ut", BF16, 64, 512, row0=64,
                                   col0=31 * 512),
                        in_=uttopT_in[:, oc * 512:(oc + 1) * 512])
                    for tt in range(16):
                        g_.wait_ge(sDV.h, int(d0 + 16 * oc + tt + 1))
                        spt[oc][tt] = gSP.dma(
                            g_, out=ffn_scr[tt * 128:(tt + 1) * 128,
                                            oc * 512:(oc + 1) * 512],
                            in_=ap(f"f2_s{['t', '1'][tt % 2]}"
                                   if False else ("f2_st" if tt % 2 == 0
                                                  else "f2_s1"),
                                   F32, 128, 512))

            @block.sync
            def _(sy):
                wait_snap(sy, snap)
                for kt in range(32):
                    mt.append(gX.dma(
                        sy, out=ap("mid", BF16, 128, T, col0=kt * T),
                        in_=mid_scr[kt * 128:(kt + 1) * 128, :]))
                mt.append(gX.dma(sy, out=ap("f2_mdl", BF16, 4, T),
                                 in_=mdl_scr))
                for oc in range(8):
                    if oc >= 1:
                        sy.wait_ge(sPE.h, int(pe0 + 16 * oc))
                    bt[oc] = gX.dma(sy, out=ap("f2_bb", F32, 1, 512),
                                    in_=fbias_in[:, oc * 512:(oc + 1) * 512])

            @block.tensor
            def _(t):
                for tok in mt:
                    gX.wait(t, tok)
                for oc in range(8):
                    for j in range(3):
                        gW.wait(t, wts[oc][j])
                    gX.wait(t, bt[oc])
                    for tt in range(16):
                        m = oc * 16 + tt
                        if m >= 4:
                            t.wait_ge(sDV.h, int(d0 + m - 3))
                        bank = pbs[tt % 4]
                        for kt in range(32):
                            t.matmul(bank[:, 0:512],
                                     ap("mid", BF16, 128, 128,
                                        col0=kt * T + tt * 128),
                                     ap("f2_ut", BF16, 128, 512,
                                        col0=kt * 512),
                                     start=(kt == 0), stop=False)
                        t.matmul(bank[:, 0:512],
                                 ap("f2_mdl", BF16, 4, 128, col0=tt * 128),
                                 ap("w2t", BF16, 4, 512, col0=oc * 512),
                                 start=False, stop=False)
                        ins = t.matmul(bank[:, 0:512],
                                       ap("c1r", F32, 1, 128),
                                       ap("f2_bb", F32, 1, 512),
                                       start=False, stop=True)
                        sPE.inc(ins)

            @block.vector
            def _(v):
                for oc in range(8):
                    for tt in range(16):
                        m = oc * 16 + tt
                        v.wait_ge(sPE.h, int(pe0 + m + 1))
                        if m >= 2 and spt[(m - 2) // 16][(m - 2) % 16]:
                            gSP.wait(v, spt[(m - 2) // 16][(m - 2) % 16])
                        ins = v.tensor_copy(
                            ap("f2_st" if tt % 2 == 0 else "f2_s1", F32, 128,
                               512), pbs[tt % 4][:, 0:512])
                        sDV.inc(ins)

        def emit_final():
            snap = snapshot()
            d0 = sDV.n
            a0 = sAC.n
            fts = []
            uts = []
            ots = [None] * 16

            def cc(i, rr=128):
                return ap("col", F32, rr, 1, col0=4 * i)

            @block.gpsimd
            def _(g_):
                for i in range(16):
                    g_.wait_ge(sDV.h, int(d0 + 6 * i + 6))
                    ots[i] = gOUT.dma(g_, out=out_t[i * 128:(i + 1) * 128, :],
                                      in_=ap(f"fn_f{i % 2}", F32, 128, DM))

            @block.sync
            def _(sy):
                wait_snap(sy, snap)
                for i in range(16):
                    if i >= 2:
                        sy.wait_ge(sDV.h, int(d0 + 6 * (i - 2) + 6))
                        gOUT.wait(sy, ots[i - 2])
                    fts.append(gX.dma(sy, out=ap(f"fn_f{i % 2}", F32, 128,
                                                 DM),
                                      in_=ffn_scr[i * 128:(i + 1) * 128, :]))
                    uts.append(gX.dma(sy, out=ap(f"fn_u{i % 2}", F32, 128,
                                                 DM),
                                      in_=u_scr[i * 128:(i + 1) * 128, :]))

            @block.scalar
            def _(s):
                for i in range(16):
                    DP = d0 + 6 * i
                    s.wait_ge(sDV.h, int(DP + 1))
                    ins = s.activation(ap("fn_j", BF16, 128, DM),
                                       ap(f"fn_f{i % 2}", F32, 128, DM),
                                       AF.Square, accum_out=cc(1))
                    sAC.inc(ins)
                    s.wait_ge(sDV.h, int(DP + 3))
                    ins = s.activation(cc(5), cc(4), AF.Ln,
                                       bias=ap("epsc", F32, 128, 1))
                    sAC.inc(ins)
                    s.wait_ge(sAC.h, sAC.n)
                    ins = s.activation(cc(5), cc(5), AF.Exp, scale=-0.5)
                    sAC.inc(ins)

            @block.vector
            def _(v):
                for i in range(16):
                    DP = d0 + 6 * i
                    AP_ = a0 + 3 * i
                    fv = ap(f"fn_f{i % 2}", F32, 128, DM)
                    uv = ap(f"fn_u{i % 2}", F32, 128, DM)
                    gX.wait(v, fts[i])
                    ins = v.tensor_scalar(fv, fv, 1.0, None, ALU.mult,
                                          ALU.add, accum_out=cc(0))
                    sDV.inc(ins)
                    v.wait_ge(sAC.h, int(AP_ + 1))
                    v.tensor_tensor(cc(3), cc(0), cc(0), ALU.mult)
                    ins = v.tensor_scalar(cc(2), cc(1), 1.0 / DM, None,
                                          ALU.mult)
                    sDV.inc(ins)
                    v.wait_ge(sDV.h, sDV.n)
                    ins = v.scalar_tensor_tensor(cc(4), cc(3),
                                                 -1.0 / (DM * DM), cc(2),
                                                 ALU.mult, ALU.add)
                    sDV.inc(ins)
                    v.wait_ge(sAC.h, int(AP_ + 3))
                    ins = v.tensor_scalar(cc(6), cc(5), cc(0), 1.0 / DM,
                                          ALU.mult, ALU.mult)
                    sDV.inc(ins)
                    v.wait_ge(sDV.h, sDV.n)
                    ins = v.tensor_scalar(fv, fv, cc(5), cc(6),
                                          ALU.mult, ALU.subtract)
                    sDV.inc(ins)
                    v.wait_ge(sDV.h, sDV.n)
                    gX.wait(v, uts[i])
                    ins = v.tensor_tensor(fv, fv, uv, ALU.add)
                    sDV.inc(ins)
                    v.wait_ge(sDV.h, sDV.n)

        phase0()
        if upto >= 1:
            emit_hprep(x_dup)
        if upto >= 2:
            emit_inproj(OT_HID + OT_B + OT_DT)
        if upto >= 3:
            emit_chain()
        if upto >= 4:
            emit_hprep(x_own)
        if upto >= 5:
            emit_inproj(OT_HID + OT_B + OT_C + OT_GATE + OT_DT)
        if upto >= 6:
            emit_scan()
        if upto >= 7:
            emit_outproj()
        if upto >= 8:
            emit_epilogue()
        if upto >= 9:
            emit_ffn1()
        if upto >= 10:
            emit_ffn2()
        if upto >= 11:
            emit_final()

        @block.gpsimd
        def _(g):
            barrier(g)
            if debug:
                for nm, s_ in [("d_h", h_scr), ("d_hid", hid_scr), ("d_b", b_scr)]:
                    gOUT.dma(g, out=dbg_outs[nm], in_=s_)
                gOUT.dma(g, out=dbg_outs["d_nwbc"], in_=ap("nwbc", F32, 128, DM))
                if upto >= 2:
                    gOUT.dma(g, out=dbg_outs["d_dt"], in_=ap("dtfm", F32, 128, T))
                    gOUT.dma(g, out=dbg_outs["d_da"], in_=ap("dafm", F32, 128, T))
                if upto >= 3:
                    gOUT.dma(g, out=dbg_outs["d_p"], in_=p_scr)
                if upto >= 5:
                    gOUT.dma(g, out=dbg_outs["d_c"], in_=c_scr)
                    gOUT.dma(g, out=dbg_outs["d_gate"], in_=gate_scr)
                if upto >= 6:
                    gOUT.dma(g, out=dbg_outs["d_y2"], in_=y2_scr)
                if upto >= 7:
                    gOUT.dma(g, out=dbg_outs["d_mo"], in_=mo_scr)
                if upto >= 8:
                    gOUT.dma(g, out=dbg_outs["d_u"], in_=u_scr)
                    gOUT.dma(g, out=dbg_outs["d_v"], in_=v_scr)
                if upto >= 9:
                    gOUT.dma(g, out=dbg_outs["d_mid"], in_=mid_scr)
                    gOUT.dma(g, out=dbg_outs["d_mdl"], in_=mdl_scr)
                if upto >= 10:
                    gOUT.dma(g, out=dbg_outs["d_ffn"], in_=ffn_scr)
            gOUT.wait_all(g)

    return nc


_NC_CACHE = {}


def _get_nc(debug, upto):
    key = (debug, upto)
    if key not in _NC_CACHE:
        _NC_CACHE[key] = build(debug=debug, upto=upto)
    return _NC_CACHE[key]


def make_in_maps(inputs):
    x = np.ascontiguousarray(inputs["L0_attn_x"], np.float32)
    shared = dict(
        wipT=np.ascontiguousarray(inputs["L0_attn_in_proj_w"].T),
        wopT=np.ascontiguousarray(inputs["L0_attn_out_proj_w"].T),
        conv_w_r=np.ascontiguousarray(
            inputs["L0_attn_conv_w"].reshape(80, 128, 4).transpose(1, 0, 2)
            .reshape(128, 320)),
        conv_b_r=np.ascontiguousarray(
            inputs["L0_attn_conv_b"].reshape(80, 128).T),
        dt_bias=np.ascontiguousarray(inputs["L0_attn_dt_bias"].reshape(128, 1)),
        a_log=np.ascontiguousarray(inputs["L0_attn_A_log"].reshape(128, 1)),
        dp_row=np.ascontiguousarray(inputs["L0_attn_Dp"].reshape(1, 128)),
        gnw_col=np.ascontiguousarray(
            inputs["L0_attn_gnorm_w"].reshape(64, 128).T),
        nw_row=np.ascontiguousarray(inputs["L0_attn_norm_w"].reshape(1, DM)),
        vhT=np.ascontiguousarray(inputs["L0_ffn_Vh_tail"].T),
        utT=np.ascontiguousarray(inputs["L0_ffn_U_tail"].T),
        vhtopT=np.ascontiguousarray(inputs["L0_ffn_Vh_top"].T),
        uttopT=np.ascontiguousarray(inputs["L0_ffn_U_top"].T),
        vhrT=np.ascontiguousarray(inputs["L0_ffn_Vh_tail_r"].T),
        urT=np.ascontiguousarray(inputs["L0_ffn_U_tail_r"].T),
        r_mat=np.ascontiguousarray(inputs["L0_ffn_R"]),
        stail_col=np.ascontiguousarray(
            np.pad(inputs["L0_ffn_S_tail"], (0, 64)).reshape(32, 128).T),
        stop_col=np.ascontiguousarray(inputs["L0_ffn_S_top"].reshape(64, 1)),
        alpha_col=np.ascontiguousarray(inputs["L0_ffn_alpha"].reshape(64, 1)),
        beta_col=np.ascontiguousarray(inputs["L0_ffn_beta"].reshape(64, 1)),
        str_col=np.ascontiguousarray(inputs["L0_ffn_S_tail_r"].reshape(4, 1)),
        dvec_col=np.ascontiguousarray(inputs["L0_ffn_D"].reshape(4, 1)),
        fbias_row=np.ascontiguousarray(inputs["L0_ffn_bias"].reshape(1, DM)),
    )
    in_maps = []
    for core in range(8):
        bi, half = core // 2, core % 2
        t0 = half * T
        xo = np.zeros((TE, DM), np.float32)
        xo[PFX:] = x[bi, t0:t0 + T]
        if t0 >= PFX:
            xo[:PFX] = x[bi, t0 - PFX:t0]
        xd = np.zeros((TE, DM), np.float32)
        xd[PFX:] = x[bi, 0:T]
        m = dict(shared)
        m["x_own"] = xo
        m["x_dup"] = xd
        m["mask_in"] = np.full((128, 1), float(half), np.float32)
        in_maps.append(m)
    return in_maps


def kernel(**inputs):
    nc = _get_nc(DEBUG, BUILD_UPTO)
    in_maps = make_in_maps(inputs)
    res = run_bass_kernel_spmd(nc, in_maps, core_ids=list(range(8)))
    out = np.zeros((4, 4096, DM), np.float32)
    for core in range(8):
        bi, half = core // 2, core % 2
        out[bi, half * T:(half + 1) * T] = res.results[core]["out"]
    return out


# version 02 (final)
# speedup vs baseline: 1.0095x; 1.0095x over previous
import numpy as np
import ml_dtypes
from contextlib import ExitStack

import concourse.bass as bass
import concourse.mybir as mybir
from concourse.bass_utils import run_bass_kernel_spmd

F32 = mybir.dt.float32
BF16 = mybir.dt.bfloat16
U8 = mybir.dt.uint8
AF = mybir.ActivationFunctionType
ALU = mybir.AluOpType
AX = mybir.AxisListType

BF16NP = ml_dtypes.bfloat16

DM = 4096
H, PD, G, ND = 128, 64, 8, 128
DI = H * PD
CONV = DI + 2 * G * ND
PROJ = DI + CONV + H
T = 2048
PFX = 16
TE = T + PFX
NCH = 16
CH = 128
EPS = 1e-5
REP = H // G

OT_HID = list(range(64, 128))
OT_B = list(range(128, 136))
OT_C = list(range(136, 144))
OT_GATE = list(range(0, 64))
OT_DT = [144]
TCHUNKS = [(0, 16), (16, 512), (528, 512), (1040, 512), (1552, 512)]

BUILD_UPTO = 11
DEBUG = False

KB = 1024
SEG_B = 31 * KB
SEG_C = 71 * KB + 512
ARENA = 205 * KB


class Sem:
    def __init__(self, nc, ctx, name):
        self.h = ctx.enter_context(nc.semaphore(name))
        self.n = 0

    def inc(self, instr, k=1):
        instr.then_inc(self.h, k)
        self.n += k
        return self.n


class DmaG:

    def __init__(self, nc, ctx, name, k):
        self.sems = [Sem(nc, ctx, f"{name}{j}") for j in range(k)]
        self.tokens = []

    def dma(self, eng, out, in_, **kw):
        s = self.sems[len(self.tokens) % len(self.sems)]
        if s.n:
            eng.wait_ge(s.h, s.n)
        ins = eng.dma_start(out=out, in_=in_, **kw)
        s.inc(ins, 16)
        self.tokens.append((s, s.n))
        return len(self.tokens) - 1

    def wait(self, eng, tok):
        s, v = self.tokens[tok]
        eng.wait_ge(s.h, v)

    def wait_all(self, eng):
        for s in self.sems:
            if s.n:
                eng.wait_ge(s.h, s.n)


def build(debug=False, upto=BUILD_UPTO):
    nc = bass.Bass("TRN2", target_bir_lowering=False)

    def din(name, shape, dtype=F32):
        return nc.dram_tensor(name, shape, dtype, kind="ExternalInput").ap()

    x_own = din("x_own", [TE, DM])
    x_dup = din("x_dup", [TE, DM])
    mask_in = din("mask_in", [128, 1])
    wipT = din("wipT", [DM, PROJ])
    wopT = din("wopT", [DI, DM])
    conv_w_in = din("conv_w_r", [128, 80 * 4])
    conv_b_in = din("conv_b_r", [128, 80])
    dt_bias_in = din("dt_bias", [128, 1])
    a_log_in = din("a_log", [128, 1])
    dp_row_in = din("dp_row", [1, 128])
    gnw_in = din("gnw_col", [128, 64])
    nw_row_in = din("nw_row", [1, DM])
    vhT = din("vhT", [DM, 4032])
    utT = din("utT", [4032, DM])
    vhtopT = din("vhtopT", [DM, 64])
    uttopT_in = din("uttopT", [64, DM])
    vhrT = din("vhrT", [DM, 4])
    urT = din("urT", [4, DM])
    r_in = din("r_mat", [4, 4])
    stail_in = din("stail_col", [128, 32])
    stop_in = din("stop_col", [64, 1])
    alpha_in = din("alpha_col", [64, 1])
    beta_in = din("beta_col", [64, 1])
    str_in = din("str_col", [4, 1])
    dvec_in = din("dvec_col", [4, 1])
    fbias_in = din("fbias_row", [1, DM])

    out_t = nc.dram_tensor("out", [T, DM], F32, kind="ExternalOutput").ap()

    def scr(name, shape, dtype):
        return nc.dram_tensor(name, shape, dtype, kind="Internal").ap()

    h_scr = scr("h_scr", [TE, DM], BF16)
    hid_scr = scr("hid_scr", [DI, T], BF16)
    b_scr = scr("b_scr", [G * ND, T], BF16)
    c_scr = scr("c_scr", [G * ND, T], BF16)
    gate_scr = scr("gate_scr", [DI, T], BF16)
    acs_scr = scr("acs_scr", [128, 128], F32)
    p_scr = scr("p_scr", [128, DI], F32)
    y2_scr = scr("y2_scr", [T, DI], BF16)
    mo_scr = scr("mo_scr", [T, DM], F32)
    u_scr = scr("u_scr", [T, DM], F32)
    v_scr = scr("v_scr", [T, DM], BF16)
    mid_scr = scr("mid_scr", [4096, T], BF16)
    mdl_scr = scr("mdl_scr", [4, T], BF16)
    ffn_scr = scr("ffn_scr", [T, DM], F32)

    dbg_outs = {}
    if debug:
        for nm, shape, dt_ in [
            ("d_h", [TE, DM], BF16), ("d_hid", [DI, T], BF16),
            ("d_b", [G * ND, T], BF16), ("d_dt", [128, T], F32),
            ("d_da", [128, T], F32), ("d_p", [128, DI], F32),
            ("d_nwbc", [128, DM], F32), ("d_y2", [T, DI], BF16),
            ("d_c", [G * ND, T], BF16), ("d_gate", [DI, T], BF16),
            ("d_mo", [T, DM], F32), ("d_u", [T, DM], F32),
            ("d_v", [T, DM], BF16), ("d_mid", [4096, T], BF16),
            ("d_mdl", [4, T], BF16), ("d_ffn", [T, DM], F32),
        ]:
            dbg_outs[nm] = nc.dram_tensor(nm, shape, dt_, kind="ExternalOutput").ap()

    jj, ii = np.meshgrid(np.arange(128), np.arange(128), indexing="ij")
    ident_bf = nc.inline_tensor(np.eye(128, dtype=BF16NP), name="ident_bf").ap()
    ident_f = nc.inline_tensor(np.eye(128, dtype=np.float32), name="ident_f").ap()
    ut_tri = nc.inline_tensor((jj <= ii).astype(np.float32), name="ut_tri").ap()
    su_tri = nc.inline_tensor((jj > ii).astype(np.float32), name="su_tri").ap()
    tri01 = nc.inline_tensor((ii >= jj).astype(np.float32), name="tri01").ap()
    allones = nc.inline_tensor(np.ones((128, 128), np.float32), name="allones").ap()
    onesrow = nc.inline_tensor(np.ones((1, 128), np.float32), name="onesrow").ap()

    est = ExitStack()
    with est:
        arena = est.enter_context(nc.sbuf_tensor("arena", [128, ARENA], U8))
        AR = arena[:, :]
        regions = {}

        def reg(name, off, nbytes):
            assert off % 32 == 0 and off + nbytes <= ARENA, (name, off, nbytes)
            regions[name] = (off, nbytes)

        def ap(name, dtype, rows, cols, row0=0, col0=0):
            off, nb = regions[name]
            esz = mybir.dt.size(dtype)
            b0 = off + col0 * esz
            assert b0 + cols * esz <= off + nb, (name, col0, cols, nb)
            return AR[row0:row0 + rows, b0:b0 + cols * esz].bitcast(dtype)

        def h3(x):
            return x.rearrange("p (h o) -> p h o", o=64)

        def w3(x, n=128):
            return x.rearrange("p (h o) -> p h o", o=1).broadcast_to([128, n, 64])

        po = 0

        def pers(name, nbytes):
            nonlocal po
            reg(name, po, nbytes)
            po = (po + nbytes + 31) & ~31
        pers("ibf", 128 * 2)
        pers("idf", 128 * 4)
        pers("cut", 128 * 4)
        pers("csu", 128 * 4)
        pers("ctr", 128 * 4)
        pers("c1s", 128 * 4)
        pers("c1r", 128 * 4)
        pers("cw", 320 * 4)
        pers("cb", 80 * 4)
        pers("dtb", 32)
        pers("acol", 32)
        pers("mskc", 32)
        pers("dpbc", 128 * 4)
        pers("gnw", 64 * 4)
        pers("stail", 32 * 4)
        pers("shead", 32)
        pers("sdc", 32)
        pers("w2t", DM * 2)
        pers("epsc", 32)
        pers("col", 64 * 4)
        pers("col2", 64 * 4)
        assert po <= 15 * KB, po

        reg("dtfm", 15 * KB, T * 4)
        reg("dafm", 23 * KB, T * 4)

        reg("hp_x0", SEG_B, DM * 4)
        reg("hp_x1", SEG_B + 16 * KB, DM * 4)
        reg("hp_hbf", SEG_B + 32 * KB, DM * 2)
        reg("ip_w0", SEG_B, 32 * 128 * 2)
        reg("ip_w1", SEG_B + 8 * KB, 32 * 128 * 2)
        reg("ip_u", SEG_B + 16 * KB, TE * 4)
        reg("ip_acc", SEG_B + 24 * KB + 512, T * 4)
        reg("ip_c0", SEG_B + 32 * KB + 512, T * 2)
        reg("ip_c1", SEG_B + 36 * KB + 512, T * 2)
        reg("ip_t1", SEG_B + 24 * KB + 512, T * 4)
        reg("ip_t2", SEG_B + 32 * KB + 512, T * 4)
        reg("sc_btm", SEG_B, 8 * 128 * 2)
        reg("sc_bfm", SEG_B + 2 * KB, 8 * 128 * 2)
        reg("sc_cfm", SEG_B + 4 * KB, 8 * 128 * 2)
        reg("sc_datm", SEG_B + 6 * KB, 128 * 4)
        reg("sc_dttm", SEG_B + 6 * KB + 512, 128 * 4)
        reg("sc_dec", SEG_B + 7 * KB, 128 * 4)
        reg("sc_etb", SEG_B + 7 * KB + 512, 128 * 4)
        reg("sc_acs", SEG_B + 8 * KB, 128 * 4)
        reg("sc_eacs", SEG_B + 8 * KB + 512, 128 * 4)
        reg("sc_w2", SEG_B + 9 * KB, 128 * 4)
        reg("sc_acsf", SEG_B + 9 * KB + 512, 128 * 4)
        reg("sc_row", SEG_B + 10 * KB, 2048 * 4)
        reg("sc_dsb", SEG_B + 18 * KB, 1024 * 4)
        reg("sc_sqj", SEG_B + 34 * KB, 512 * 4)
        reg("sc_gm", SEG_B + 22 * KB, 128 * 4)
        reg("sc_mt", SEG_B + 23 * KB, 8 * 128 * 2)
        reg("sc_t1", SEG_B + 25 * KB, 512 * 4)
        reg("sc_sg", SEG_B + 27 * KB, 512 * 4)
        reg("sc_yg", SEG_B + 29 * KB, 1024 * 4)
        reg("sc_dph", SEG_B + 33 * KB, 512 * 2)
        reg("f1_w0", SEG_B, 32 * 128 * 2)
        reg("f1_w1", SEG_B + 8 * KB, 32 * 128 * 2)
        reg("f1_md0", SEG_B + 16 * KB, T * 2)
        reg("f1_md1", SEG_B + 20 * KB, T * 2)
        reg("f2_ut", SEG_B, 32 * 512 * 2)
        reg("f2_st", SEG_B + 32 * KB, 512 * 4)
        reg("f2_bb", SEG_B + 34 * KB, 512 * 4)
        reg("f2_s1", SEG_B + 36 * KB, 512 * 4)

        reg("rowtmp", SEG_C, DM * 4)
        reg("nwbc", SEG_C + 16 * KB, DM * 4)
        reg("hfm", SEG_C, 32 * TE * 2)
        reg("sc_hid", SEG_C, DI * 2)
        reg("sc_hd2big", SEG_C + 16 * KB, DI * 2)
        reg("pst", SEG_C + 32 * KB, DI * 4)
        reg("sc_gate", SEG_C + 64 * KB, DI * 2)
        reg("sc_hdbig", SEG_C + 80 * KB, DI * 2)
        reg("sc_y2", SEG_C + 96 * KB, DI * 2)
        reg("sc_pbf", SEG_C + 112 * KB, DI * 2)
        reg("ur_f", SEG_C + 32 * KB, DM * 4)
        reg("op_ya", SEG_C, 1024 * 2)
        reg("op_yb", SEG_C + 2 * KB, 1024 * 2)
        reg("op_s0", SEG_C + 4 * KB, 512 * 4)
        reg("op_s1", SEG_C + 6 * KB, 512 * 4)
        reg("op_w", SEG_C + 64 * KB, 64 * 512 * 2)
        reg("me_x0", SEG_C, DM * 4)
        reg("me_x1", SEG_C + 16 * KB, DM * 4)
        reg("me_mo0", SEG_C + 32 * KB, DM * 4)
        reg("me_mo1", SEG_C + 48 * KB, DM * 4)
        reg("me_u", SEG_C + 64 * KB, DM * 4)
        reg("me_v", SEG_C + 80 * KB, DM * 2)
        reg("me_j", SEG_C + 88 * KB, DM * 2)
        reg("vfm", SEG_C, 32 * T * 2)
        reg("mid", SEG_C, 32 * T * 2)
        reg("f2_mdl", SEG_C + 128 * KB, T * 2)
        reg("fn_f0", SEG_C, DM * 4)
        reg("fn_f1", SEG_C + 16 * KB, DM * 4)
        reg("fn_u0", SEG_C + 32 * KB, DM * 4)
        reg("fn_u1", SEG_C + 48 * KB, DM * 4)
        reg("fn_j", SEG_C + 64 * KB, DM * 2)

        pbs = [est.enter_context(nc.psum_tensor(f"pb{i}", [128, 512], F32))
               for i in range(8)]

        gX = DmaG(nc, est, "gX", 2)
        gW = DmaG(nc, est, "gW", 2)
        gSP = DmaG(nc, est, "gSP", 4)
        gTD = DmaG(nc, est, "gTD", 4)
        gOUT = DmaG(nc, est, "gOUT", 2)
        sPE = Sem(nc, est, "sPE")
        sDV = Sem(nc, est, "sDV")
        sAC = Sem(nc, est, "sAC")
        ALL_SEMS = [sPE, sDV, sAC]
        ALL_GRPS = [gX, gW, gSP, gTD, gOUT]

        block = est.enter_context(nc.Block())

        def barrier(eng):
            for s_ in ALL_SEMS:
                if s_.n:
                    eng.wait_ge(s_.h, s_.n)
            for g_ in ALL_GRPS:
                g_.wait_all(eng)

        def snapshot():
            snap = [(s_.h, s_.n) for s_ in ALL_SEMS]
            for g_ in ALL_GRPS:
                snap += [(s_.h, s_.n) for s_ in g_.sems]
            return snap

        def wait_snap(eng, snap):
            for h, n in snap:
                if n:
                    eng.wait_ge(h, n)

        def phase0():
            @block.sync
            def _(sy):
                for dst, src in [
                    (ap("ibf", BF16, 128, 128), ident_bf),
                    (ap("idf", F32, 128, 128), ident_f),
                    (ap("cut", F32, 128, 128), ut_tri),
                    (ap("csu", F32, 128, 128), su_tri),
                    (ap("ctr", F32, 128, 128), tri01),
                    (ap("c1s", F32, 128, 128), allones),
                    (ap("c1r", F32, 1, 128), onesrow),
                    (ap("cw", F32, 128, 320), conv_w_in),
                    (ap("cb", F32, 128, 80), conv_b_in),
                    (ap("dtb", F32, 128, 1), dt_bias_in),
                    (ap("col", F32, 128, 1), a_log_in),
                    (ap("mskc", F32, 128, 1), mask_in),
                    (ap("gnw", F32, 128, 64), gnw_in),
                    (ap("stail", F32, 128, 32), stail_in),
                    (ap("rowtmp", F32, 1, 128), dp_row_in),
                    (ap("col2", F32, 64, 1), stop_in),
                    (ap("col2", F32, 64, 1, col0=1), alpha_in),
                    (ap("col2", F32, 64, 1, col0=2), beta_in),
                    (ap("col2", F32, 4, 1, col0=3), str_in),
                    (ap("col2", F32, 4, 1, col0=4), dvec_in),
                    (ap("col2", F32, 4, 4, col0=5), r_in),
                    (ap("ur_f", F32, 4, DM), urT),
                ]:
                    gX.dma(sy, out=dst, in_=src)

            @block.scalar
            def _(s):
                gX.wait_all(s)
                ins = s.activation(ap("acol", F32, 128, 1), ap("col", F32, 128, 1),
                                   AF.Exp)
                sAC.inc(ins)

            @block.vector
            def _(v):
                gX.wait_all(v)
                v.wait_ge(sAC.h, sAC.n)
                v.memset(ap("epsc", F32, 128, 1), EPS)
                v.tensor_scalar(ap("acol", F32, 128, 1), ap("acol", F32, 128, 1),
                                -1.0, None, ALU.mult)
                ins = v.tensor_scalar(ap("shead", F32, 64, 1), ap("col2", F32, 64, 1),
                                      ap("col2", F32, 64, 1, col0=1),
                                      ap("col2", F32, 64, 1, col0=2),
                                      ALU.mult, ALU.add)
                sDV.inc(ins)
                v.wait_ge(sDV.h, sDV.n)
                ins = v.tensor_scalar(ap("shead", F32, 64, 1), ap("shead", F32, 64, 1),
                                      0.0, None, ALU.max)
                sDV.inc(ins)
                v.wait_ge(sDV.h, sDV.n)
                ins = v.tensor_scalar(ap("sdc", F32, 4, 1),
                                      ap("col2", F32, 4, 1, col0=4),
                                      0.0, None, ALU.max)
                sDV.inc(ins)
                v.wait_ge(sDV.h, sDV.n)
                ins = v.tensor_tensor(ap("sdc", F32, 4, 1), ap("sdc", F32, 4, 1),
                                      ap("col2", F32, 4, 1, col0=3), ALU.mult)
                sDV.inc(ins)
                v.wait_ge(sDV.h, sDV.n)
                ins = v.tensor_scalar(ap("ur_f", F32, 4, DM), ap("ur_f", F32, 4, DM),
                                      ap("sdc", F32, 4, 1), None, ALU.mult)
                sDV.inc(ins)

            pe0 = sPE.n
            dv0 = sDV.n

            @block.tensor
            def _(t):
                t.wait_ge(sDV.h, dv0)
                ins = t.matmul(pbs[0][:, 0:128], ap("c1r", F32, 1, 128),
                               ap("rowtmp", F32, 1, 128), start=True, stop=True)
                sPE.inc(ins)
                for j in range(8):
                    if j >= 2:
                        t.wait_ge(sDV.h, dv0 + j)
                    ins = t.matmul(pbs[2 + j % 2][0:4, 0:512],
                                   ap("col2", F32, 4, 4, col0=5),
                                   ap("ur_f", F32, 4, 512, col0=j * 512),
                                   start=True, stop=True)
                    sPE.inc(ins)

            @block.vector
            def _(v):
                v.wait_ge(sPE.h, pe0 + 1)
                ins = v.tensor_copy(ap("dpbc", F32, 128, 128), pbs[0][:, 0:128])
                sDV.inc(ins)
                for j in range(8):
                    v.wait_ge(sPE.h, pe0 + 2 + j)
                    ins = v.tensor_copy(ap("w2t", BF16, 4, 512, col0=j * 512),
                                        pbs[2 + j % 2][0:4, 0:512])
                    sDV.inc(ins)

        def emit_nwbc():
            pe0 = sPE.n
            dv0 = sDV.n
            toks = []

            @block.sync
            def _(sy):
                barrier(sy)
                toks.append(gX.dma(sy, out=ap("rowtmp", F32, 1, DM), in_=nw_row_in))

            @block.tensor
            def _(t):
                gX.wait(t, toks[0])
                for j in range(8):
                    if j >= 2:
                        t.wait_ge(sDV.h, dv0 + j - 1)
                    ins = t.matmul(pbs[j % 2][:, 0:512], ap("c1r", F32, 1, 128),
                                   ap("rowtmp", F32, 1, 512, col0=j * 512),
                                   start=True, stop=True)
                    sPE.inc(ins)

            @block.vector
            def _(v):
                for j in range(8):
                    v.wait_ge(sPE.h, pe0 + j + 1)
                    ins = v.tensor_copy(ap("nwbc", F32, 128, 512, col0=j * 512),
                                        pbs[j % 2][:, 0:512])
                    sDV.inc(ins)

        def emit_hprep(x_ap):
            emit_nwbc()
            ttiles = [(i * 128, 128) for i in range(16)] + [(2048, 16)]
            a0 = sAC.n
            d0 = sDV.n
            xt = []
            spt = []

            def cc(i, rr=128):
                return ap("col", F32, rr, 1, col0=4 * i)

            @block.sync
            def _(sy):
                barrier(sy)
                for i, (r0, rr) in enumerate(ttiles):
                    if i >= 2:
                        sy.wait_ge(sDV.h, d0 + 8 * (i - 2) + 8)
                    xt.append(gX.dma(sy, out=ap(f"hp_x{i % 2}", F32, rr, DM),
                                     in_=x_ap[r0:r0 + rr, :]))

            @block.gpsimd
            def _(g):
                for i, (r0, rr) in enumerate(ttiles):
                    g.wait_ge(sDV.h, d0 + 8 * i + 8)
                    spt.append(gSP.dma(g, out=h_scr[r0:r0 + rr, :],
                                       in_=ap("hp_hbf", BF16, rr, DM)))

            @block.scalar
            def _(s):
                for i, (r0, rr) in enumerate(ttiles):
                    xv = ap(f"hp_x{i % 2}", F32, rr, DM)
                    gX.wait(s, xt[i])
                    s.wait_ge(sDV.h, d0 + 8 * i + 1)
                    if i >= 1:
                        gSP.wait(s, spt[i - 1])
                    ins = s.activation(ap("hp_hbf", BF16, rr, DM), xv, AF.Square,
                                       accum_out=cc(1, rr))
                    sAC.inc(ins)
                    s.wait_ge(sDV.h, d0 + 8 * i + 3)
                    ins = s.activation(cc(5, rr), cc(4, rr), AF.Ln,
                                       bias=ap("epsc", F32, rr, 1))
                    sAC.inc(ins)
                    s.wait_ge(sAC.h, sAC.n)
                    ins = s.activation(cc(5, rr), cc(5, rr), AF.Exp, scale=-0.5)
                    sAC.inc(ins)
                    s.wait_ge(sDV.h, d0 + 8 * i + 4)
                    ins = s.activation(cc(7, rr), cc(6, rr), AF.Ln,
                                       bias=ap("epsc", F32, rr, 1))
                    sAC.inc(ins)
                    s.wait_ge(sAC.h, sAC.n)
                    ins = s.activation(cc(7, rr), cc(7, rr), AF.Exp, scale=-0.5)
                    sAC.inc(ins)

            @block.vector
            def _(v):
                for i, (r0, rr) in enumerate(ttiles):
                    xv = ap(f"hp_x{i % 2}", F32, rr, DM)
                    gX.wait(v, xt[i])
                    if i >= 1:
                        v.wait_ge(sAC.h, a0 + 5 * (i - 1) + 5)
                    ins = v.tensor_scalar(xv, xv, 1.0, None, ALU.mult, ALU.add,
                                          accum_out=cc(0, rr))
                    sDV.inc(ins)
                    v.wait_ge(sAC.h, a0 + 5 * i + 1)
                    v.tensor_tensor(cc(3, rr), cc(0, rr), cc(0, rr), ALU.mult)
                    ins = v.tensor_scalar(cc(2, rr), cc(1, rr), 1.0 / DM, None,
                                          ALU.mult)
                    sDV.inc(ins)
                    v.wait_ge(sDV.h, d0 + 8 * i + 2)
                    ins = v.scalar_tensor_tensor(cc(4, rr), cc(3, rr),
                                                 -1.0 / (DM * DM), cc(2, rr),
                                                 ALU.mult, ALU.add)
                    sDV.inc(ins)
                    v.wait_ge(sAC.h, a0 + 5 * i + 3)
                    ins = v.tensor_scalar(cc(6, rr), cc(4, rr), cc(5, rr), cc(5, rr),
                                          ALU.mult, ALU.mult)
                    sDV.inc(ins)
                    v.wait_ge(sAC.h, a0 + 5 * i + 5)
                    ins = v.tensor_scalar(cc(8, rr), cc(5, rr), cc(7, rr), None,
                                          ALU.mult)
                    sDV.inc(ins)
                    v.wait_ge(sDV.h, d0 + 8 * i + 5)
                    ins = v.tensor_scalar(cc(9, rr), cc(8, rr), cc(0, rr),
                                          1.0 / DM, ALU.mult, ALU.mult)
                    sDV.inc(ins)
                    v.wait_ge(sDV.h, d0 + 8 * i + 6)
                    ins = v.tensor_scalar(xv, xv, cc(8, rr), cc(9, rr),
                                          ALU.mult, ALU.subtract)
                    sDV.inc(ins)
                    v.wait_ge(sDV.h, d0 + 8 * i + 7)
                    if i >= 1:
                        gSP.wait(v, spt[i - 1])
                    ins = v.tensor_tensor(ap("hp_hbf", BF16, rr, DM), xv,
                                          ap("nwbc", F32, rr, DM), ALU.mult)
                    sDV.inc(ins)

        def emit_inproj(tiles):
            hft = []

            @block.sync
            def _(sy):
                barrier(sy)
                for kt in range(32):
                    hft.append(gTD.dma(
                        sy, out=ap("hfm", BF16, 128, TE, col0=kt * TE),
                        in_=h_scr[:, kt * 128:(kt + 1) * 128], transpose=True))

            pe0 = sPE.n
            a0 = sAC.n
            d0 = sDV.n
            nt = len(tiles)

            def kindof(ot):
                return "dt" if ot in OT_DT else ("gate" if ot < 64 else "conv")

            kinds = [kindof(ot) for ot in tiles]
            acts_per = [6 if k == "conv" else (5 if k == "gate" else 8)
                        for k in kinds]
            act_cum = np.concatenate([[0], np.cumsum(acts_per)]).astype(int)
            dvs_per = [4 if k == "conv" else (1 if k == "gate" else 4)
                       for k in kinds]
            dv_cum = np.concatenate([[0], np.cumsum(dvs_per)]).astype(int)
            sp_per = [1 if k != "dt" else 0 for k in kinds]
            evac_done = []
            for i in range(nt):
                for ci in range(5):
                    evac_done.append(act_cum[i] + ci + 1)
            post_sem = [sAC if k == "conv" else sDV for k in kinds]
            post_cnt = [int(a0 + act_cum[i] + 6) if kinds[i] == "conv"
                        else int(d0 + dv_cum[i] + 1) for i in range(nt)]
            wt = []
            spt = {}

            def spill_dst(ot):
                if ot < 64:
                    return gate_scr[ot * 128:(ot + 1) * 128, :]
                if ot < 128:
                    return hid_scr[(ot - 64) * 128:(ot - 63) * 128, :]
                if ot < 136:
                    return b_scr[(ot - 128) * 128:(ot - 127) * 128, :]
                return c_scr[(ot - 136) * 128:(ot - 135) * 128, :]

            @block.gpsimd
            def _(g):
                gTD.wait_all(g)
                for i, ot in enumerate(tiles):
                    if i >= 2:
                        g.wait_ge(sPE.h, pe0 + 5 * (i - 1))
                    wt.append(gW.dma(
                        g, out=ap(f"ip_w{i % 2}", BF16, 128, 32 * 128),
                        in_=wipT[:, ot * 128:(ot + 1) * 128].rearrange(
                            "(kt p) o -> p kt o", p=128)))
                    if i >= 1 and sp_per[i - 1]:
                        g.wait_ge(post_sem[i - 1].h, post_cnt[i - 1])
                        spt[i - 1] = gSP.dma(
                            g, out=spill_dst(tiles[i - 1]),
                            in_=ap(f"ip_c{(i - 1) % 2}", BF16, 128, T))
                i = nt - 1
                if sp_per[i]:
                    g.wait_ge(post_sem[i].h, post_cnt[i])
                    spt[i] = gSP.dma(g, out=spill_dst(tiles[i]),
                                     in_=ap(f"ip_c{i % 2}", BF16, 128, T))

            @block.tensor
            def _(t):
                gTD.wait_all(t)
                m = 0
                for i, ot in enumerate(tiles):
                    gW.wait(t, wt[i])
                    wv = ap(f"ip_w{i % 2}", BF16, 128, 32 * 128)
                    for (c0, csz) in TCHUNKS:
                        bank = pbs[m % 4]
                        if m >= 4:
                            t.wait_ge(sAC.h, int(a0 + evac_done[m - 4]))
                        for k in range(32):
                            ins = t.matmul(
                                bank[:, 0:csz],
                                wv[:, k * 128:(k + 1) * 128],
                                ap("hfm", BF16, 128, csz, col0=k * TE + c0),
                                start=(k == 0), stop=(k == 31))
                        sPE.inc(ins)
                        m += 1

            @block.scalar
            def _(s):
                m = 0
                for i, ot in enumerate(tiles):
                    kind = kinds[i]
                    if i >= 1:
                        s.wait_ge(sDV.h, int(d0 + dv_cum[i]))
                    for ci, (c0, csz) in enumerate(TCHUNKS):
                        s.wait_ge(sPE.h, pe0 + m + 1)
                        ins = s.copy(ap("ip_u", F32, 128, csz, col0=c0),
                                     pbs[m % 4][:, 0:csz])
                        sAC.inc(ins)
                        m += 1
                    if kind == "conv":
                        s.wait_ge(sDV.h, int(d0 + dv_cum[i + 1]))
                        if i >= 2 and sp_per[i - 2]:
                            gSP.wait(s, spt[i - 2])
                        ins = s.activation(
                            ap(f"ip_c{i % 2}", BF16, 128, T),
                            ap("ip_acc", F32, 128, T), AF.Silu,
                            bias=ap("cb", F32, 128, 1, col0=ot - 64))
                        sAC.inc(ins)
                    elif kind == "dt":
                        s.wait_ge(sDV.h, int(d0 + dv_cum[i] + 1))
                        gSP.wait_all(s)
                        ins = s.activation(ap("ip_t2", F32, 128, T),
                                           ap("ip_t1", F32, 128, T), AF.Abs)
                        sAC.inc(ins)
                        s.wait_ge(sAC.h, sAC.n)
                        ins = s.activation(ap("ip_t2", F32, 128, T),
                                           ap("ip_t2", F32, 128, T), AF.Exp,
                                           scale=-1.0)
                        sAC.inc(ins)
                        s.wait_ge(sAC.h, sAC.n)
                        ins = s.activation(ap("ip_t2", F32, 128, T),
                                           ap("ip_t2", F32, 128, T), AF.Ln, bias=1.0)
                        sAC.inc(ins)

            @block.vector
            def _(v):
                for i, ot in enumerate(tiles):
                    kind = kinds[i]
                    v.wait_ge(sAC.h, int(a0 + act_cum[i] + 5))
                    if kind == "conv":
                        accv = ap("ip_acc", F32, 128, T)

                        def uv(o0):
                            return ap("ip_u", F32, 128, T, col0=o0)

                        def wc(tap):
                            return ap("cw", F32, 128, 1, col0=(ot - 64) * 4 + tap)
                        ins = v.tensor_scalar(accv, uv(13), wc(0), None, ALU.mult)
                        sDV.inc(ins)
                        v.wait_ge(sDV.h, sDV.n)
                        ins = v.scalar_tensor_tensor(accv, uv(14), wc(1), accv,
                                                     ALU.mult, ALU.add)
                        sDV.inc(ins)
                        v.wait_ge(sDV.h, sDV.n)
                        ins = v.scalar_tensor_tensor(accv, uv(15), wc(2), accv,
                                                     ALU.mult, ALU.add)
                        sDV.inc(ins)
                        v.wait_ge(sDV.h, sDV.n)
                        ins = v.scalar_tensor_tensor(accv, uv(16), wc(3), accv,
                                                     ALU.mult, ALU.add)
                        sDV.inc(ins)
                    elif kind == "gate":
                        if i >= 2 and sp_per[i - 2]:
                            gSP.wait(v, spt[i - 2])
                        ins = v.tensor_copy(ap(f"ip_c{i % 2}", BF16, 128, T),
                                            ap("ip_u", F32, 128, T, col0=PFX))
                        sDV.inc(ins)
                    else:
                        ins = v.tensor_scalar(ap("ip_t1", F32, 128, T),
                                              ap("ip_u", F32, 128, T, col0=PFX),
                                              ap("dtb", F32, 128, 1), None, ALU.add)
                        sDV.inc(ins)
                        v.wait_ge(sAC.h, int(a0 + act_cum[i] + 8))
                        ins = v.tensor_scalar(ap("ip_t1", F32, 128, T),
                                              ap("ip_t1", F32, 128, T),
                                              0.0, None, ALU.max)
                        sDV.inc(ins)
                        v.wait_ge(sDV.h, sDV.n)
                        ins = v.tensor_tensor(ap("dtfm", F32, 128, T),
                                              ap("ip_t1", F32, 128, T),
                                              ap("ip_t2", F32, 128, T), ALU.add)
                        sDV.inc(ins)
                        v.wait_ge(sDV.h, sDV.n)
                        ins = v.tensor_scalar(ap("dafm", F32, 128, T),
                                              ap("dtfm", F32, 128, T),
                                              ap("acol", F32, 128, 1), None, ALU.mult)
                        sDV.inc(ins)

        def emit_chain():
            @block.vector
            def _(v):
                barrier(v)
                ins = v.memset(ap("pst", F32, 128, DI), 0.0)
                sDV.inc(ins)

            pe0 = sPE.n
            d0 = sDV.n
            a0 = sAC.n
            ht = []
            bt = []

            PE_C = 19
            DV_C = 21

            @block.sync
            def _(sy):
                barrier(sy)
                for c in range(NCH):
                    if c >= 1:
                        sy.wait_ge(sDV.h, d0 + DV_C * (c - 1) + 4)
                    ht.append(gTD.dma(sy, out=ap("sc_hid", BF16, 128, DI),
                                      in_=hid_scr[:, c * CH:(c + 1) * CH],
                                      transpose=True))
                    if c >= 1:
                        sy.wait_ge(sPE.h, pe0 + PE_C * (c - 1) + 3)
                    bt.append(gX.dma(
                        sy, out=ap("sc_bfm", BF16, 128, 8 * 128),
                        in_=b_scr.rearrange("(g n) l -> n g l", n=128)[
                            :, :, c * CH:(c + 1) * CH]))

            @block.tensor
            def _(t):
                t.wait_ge(sDV.h, d0)
                t.wait_ge(sAC.h, a0)
                for c in range(NCH):
                    if c >= 1:
                        t.wait_ge(sAC.h, a0 + 2 * c)
                        t.wait_ge(sDV.h, d0 + DV_C * (c - 1) + 1)
                    t.transpose(pbs[6][:, 0:128],
                                ap("dafm", F32, 128, 128, col0=c * CH),
                                ap("idf", F32, 128, 128))
                    ins = t.transpose(pbs[7][:, 0:128],
                                      ap("dtfm", F32, 128, 128, col0=c * CH),
                                      ap("idf", F32, 128, 128))
                    sPE.inc(ins)
                    t.wait_ge(sDV.h, d0 + DV_C * c + 1)
                    t.matmul(pbs[6][:, 0:128], ap("csu", F32, 128, 128),
                             ap("sc_datm", F32, 128, 128), start=True, stop=True)
                    ins = t.matmul(pbs[7][:, 0:128], ap("c1s", F32, 128, 128),
                                   ap("sc_datm", F32, 128, 128),
                                   start=True, stop=True)
                    sPE.inc(ins)
                    gX.wait(t, bt[c])
                    if c >= 1:
                        t.wait_ge(sDV.h, d0 + DV_C * (c - 1) + 2)
                    for gi in range(8):
                        bank = pbs[0].ap().bitcast(BF16)
                        ins = t.transpose(bank[:, gi * 128:(gi + 1) * 128],
                                          ap("sc_bfm", BF16, 128, 128,
                                             col0=gi * 128),
                                          ap("ibf", BF16, 128, 128))
                    sPE.inc(ins)
                    t.wait_ge(sDV.h, d0 + DV_C * c + 4)
                    for f in range(16):
                        bank = pbs[[2, 3, 4, 5][f % 4]]
                        if f >= 4:
                            t.wait_ge(sDV.h, d0 + DV_C * c + 2 + f)
                        elif c >= 1:
                            t.wait_ge(sDV.h, d0 + DV_C * (c - 1) + 18 + f)
                        for hh in range(8):
                            hd = f * 8 + hh
                            ins = t.matmul(
                                bank[:, hh * 64:(hh + 1) * 64],
                                ap("sc_btm", BF16, 128, 128,
                                   col0=(hd // 16) * 128),
                                ap("sc_hd2big", BF16, 128, 64, col0=hd * 64),
                                start=True, stop=True)
                        sPE.inc(ins)

            @block.scalar
            def _(s):
                for c in range(NCH):
                    if c >= 1:
                        s.wait_ge(sDV.h, d0 + DV_C * (c - 1) + 5)
                    s.wait_ge(sPE.h, pe0 + PE_C * c + 2)
                    ins = s.activation(ap("sc_dec", F32, 128, 128), pbs[6][:, 0:128],
                                       AF.Exp)
                    sAC.inc(ins)
                    ins = s.activation(ap("sc_etb", F32, 128, 128), pbs[7][:, 0:128],
                                       AF.Exp)
                    sAC.inc(ins)

            @block.vector
            def _(v):
                for c in range(NCH):
                    v.wait_ge(sPE.h, pe0 + PE_C * c + 1)
                    v.tensor_copy(ap("sc_datm", F32, 128, 128), pbs[6][:, 0:128])
                    ins = v.tensor_copy(ap("sc_dttm", F32, 128, 128),
                                        pbs[7][:, 0:128])
                    sDV.inc(ins)
                    v.wait_ge(sPE.h, pe0 + PE_C * c + 3)
                    ins = v.tensor_copy(ap("sc_btm", BF16, 128, 1024),
                                        pbs[0].ap().bitcast(BF16)[:, 0:1024])
                    sDV.inc(ins)
                    v.wait_ge(sAC.h, a0 + 2 * c + 1)
                    ins = v.tensor_tensor(ap("sc_w2", F32, 128, 128),
                                          ap("sc_dttm", F32, 128, 128),
                                          ap("sc_dec", F32, 128, 128), ALU.mult)
                    sDV.inc(ins)
                    v.wait_ge(sDV.h, d0 + DV_C * c + 3)
                    gTD.wait(v, ht[c])
                    ins = v.tensor_tensor(h3(ap("sc_hd2big", BF16, 128, DI)),
                                          h3(ap("sc_hid", BF16, 128, DI)),
                                          w3(ap("sc_w2", F32, 128, 128)),
                                          ALU.mult)
                    sDV.inc(ins)
                    v.wait_ge(sAC.h, a0 + 2 * c + 2)
                    if c >= 1:
                        v.wait_ge(sDV.h, d0 + DV_C * (c - 1) + 21)
                    else:
                        v.wait_ge(sDV.h, d0)
                    ins = v.tensor_tensor(h3(ap("pst", F32, 128, DI)),
                                          h3(ap("pst", F32, 128, DI)),
                                          w3(ap("sc_etb", F32, 128, 128)),
                                          ALU.mult)
                    sDV.inc(ins)
                    v.wait_ge(sDV.h, d0 + DV_C * c + 5)
                    for f in range(16):
                        bank = pbs[[2, 3, 4, 5][f % 4]]
                        v.wait_ge(sPE.h, pe0 + PE_C * c + 4 + f)
                        ins = v.tensor_tensor(
                            ap("pst", F32, 128, 512, col0=f * 512),
                            ap("pst", F32, 128, 512, col0=f * 512),
                            bank[:, 0:512], ALU.add)
                        sDV.inc(ins)

            @block.vector
            def _(v):
                v.wait_ge(sDV.h, d0 + DV_C * NCH)
                ins = v.tensor_scalar(ap("pst", F32, 128, DI), ap("pst", F32, 128, DI),
                                      ap("mskc", F32, 128, 1), None, ALU.mult)
                sDV.inc(ins)

            @block.gpsimd
            def _(g):
                g.wait_ge(sDV.h, sDV.n)
                gSP.dma(g, out=p_scr, in_=ap("pst", F32, 128, DI))

        def emit_scan():
            pe0 = sPE.n
            d0 = sDV.n
            a0 = sAC.n
            PE_C2, DV_C2, AC_C2 = 77, 160, 68

            def PEc(c, k):
                return int(pe0 + PE_C2 * c + k)

            def DVc(c, k):
                return int(d0 + DV_C2 * c + k)

            def ACc(c, k):
                return int(a0 + AC_C2 * c + k)

            def cc(i):
                return ap("col", F32, 128, 1, col0=4 * i)

            ht = []
            gt = []
            bt = []
            ct = []
            t_pst = []
            tok_acs = [None] * NCH
            tok_row = [[None] * 8 for _ in range(NCH)]
            tok_y2 = [None] * NCH

            @block.sync
            def _(sy):
                barrier(sy)
                t_pst.append(gX.dma(sy, out=ap("pst", F32, 128, DI), in_=p_scr))
                for c in range(NCH):
                    csl = slice(c * CH, (c + 1) * CH)
                    if c >= 1:
                        sy.wait_ge(sDV.h, DVc(c - 1, 143))
                    ht.append(gTD.dma(sy, out=ap("sc_hid", BF16, 128, DI),
                                      in_=hid_scr[:, csl], transpose=True))
                    if c >= 1:
                        sy.wait_ge(sAC.h, ACc(c - 1, 66))
                    gt.append(gTD.dma(sy, out=ap("sc_gate", BF16, 128, DI),
                                      in_=gate_scr[:, csl], transpose=True))
                    if c >= 1:
                        sy.wait_ge(sPE.h, PEc(c - 1, 55))
                    bt.append(gX.dma(
                        sy, out=ap("sc_bfm", BF16, 128, 8 * 128),
                        in_=b_scr.rearrange("(g n) l -> n g l", n=128)[:, :, csl]))
                    if c >= 1:
                        sy.wait_ge(sPE.h, PEc(c - 1, 61))
                    ct.append(gX.dma(
                        sy, out=ap("sc_cfm", BF16, 128, 8 * 128),
                        in_=c_scr.rearrange("(g n) l -> n g l", n=128)[:, :, csl]))

            @block.gpsimd
            def _(g_):
                for c in range(NCH):
                    g_.wait_ge(sDV.h, DVc(c, 3))
                    tok_acs[c] = gSP.dma(g_, out=acs_scr,
                                         in_=ap("sc_acsf", F32, 128, 128))
                    for g in range(8):
                        if g == 0:
                            gSP.wait(g_, tok_acs[c])
                            if c >= 1:
                                g_.wait_ge(sPE.h, PEc(c - 1, 59))
                        else:
                            g_.wait_ge(sPE.h, PEc(c, 7 * g + 3))
                        tok_row[c][g] = gSP.dma(
                            g_, out=ap("sc_row", F32, 1, 2048),
                            in_=acs_scr[16 * g:16 * (g + 1), :])
                    g_.wait_ge(sDV.h, DVc(c, 141))
                    tok_y2[c] = gSP.dma(g_, out=y2_scr[c * CH:(c + 1) * CH, :],
                                        in_=ap("sc_y2", BF16, 128, DI))

            @block.tensor
            def _(t):
                for c in range(NCH):
                    if c == 0:
                        t.wait_ge(sDV.h, d0)
                        t.wait_ge(sAC.h, a0)
                    else:
                        t.wait_ge(sDV.h, DVc(c - 1, 160))
                        t.wait_ge(sAC.h, ACc(c - 1, 3))
                    t.transpose(pbs[6][:, 0:128],
                                ap("dafm", F32, 128, 128, col0=c * CH),
                                ap("idf", F32, 128, 128))
                    ins = t.transpose(pbs[7][:, 0:128],
                                      ap("dtfm", F32, 128, 128, col0=c * CH),
                                      ap("idf", F32, 128, 128))
                    sPE.inc(ins)
                    t.wait_ge(sDV.h, DVc(c, 1))
                    t.matmul(pbs[6][:, 0:128], ap("cut", F32, 128, 128),
                             ap("sc_datm", F32, 128, 128), start=True, stop=True)
                    ins = t.matmul(pbs[7][:, 0:128], ap("csu", F32, 128, 128),
                                   ap("sc_datm", F32, 128, 128),
                                   start=True, stop=True)
                    sPE.inc(ins)
                    t.wait_ge(sDV.h, DVc(c, 2))
                    ins = t.matmul(pbs[6][:, 0:128], ap("c1s", F32, 128, 128),
                                   ap("sc_datm", F32, 128, 128),
                                   start=True, stop=True)
                    sPE.inc(ins)
                    t.wait_ge(sAC.h, ACc(c, 2))
                    ins = t.transpose(pbs[7][:, 0:128],
                                      ap("sc_acs", F32, 128, 128),
                                      ap("idf", F32, 128, 128))
                    sPE.inc(ins)
                    gX.wait(t, bt[c])
                    for gi in range(8):
                        bank = pbs[0].ap().bitcast(BF16)
                        ins = t.transpose(bank[:, gi * 128:(gi + 1) * 128],
                                          ap("sc_bfm", BF16, 128, 128,
                                             col0=gi * 128),
                                          ap("ibf", BF16, 128, 128))
                    sPE.inc(ins)
                    gX.wait(t, ct[c])
                    for g in range(8):
                        if g == 0:
                            t.wait_ge(sDV.h, DVc(c, 3))
                        else:
                            t.wait_ge(sDV.h, DVc(c, 6 + 17 * (g - 1)))
                        ins = t.matmul(pbs[7][:, 0:128],
                                       ap("sc_bfm", BF16, 128, 128, col0=g * 128),
                                       ap("sc_cfm", BF16, 128, 128, col0=g * 128),
                                       start=True, stop=True)
                        sPE.inc(ins)
                        for si in range(2):
                            h0 = 16 * g + 8 * si
                            gSP.wait(t, tok_row[c][g])
                            if not (g == 0 and si == 0):
                                prev_stt = (DVc(c, 7 + 17 * g) if si == 1
                                            else DVc(c, 14 + 17 * (g - 1)))
                                t.wait_ge(sDV.h, prev_stt)
                            for q in range(2):
                                ins = t.matmul(
                                    pbs[1 + 4 * q][:, 0:512],
                                    ap("c1r", F32, 1, 128),
                                    ap("sc_row", F32, 1, 512,
                                       col0=(8 * si + 4 * q) * 128),
                                    start=True, stop=True)
                            sPE.inc(ins)
                            t.wait_ge(sDV.h, DVc(c, 9 + 17 * g + 7 * si))
                            if not (g == 0 and si == 0):
                                prev_y8 = (DVc(c, 11 + 17 * g) if si == 1
                                           else DVc(c, 18 + 17 * (g - 1)))
                                t.wait_ge(sDV.h, prev_y8)
                            t.matmul(pbs[2][:, 0:512],
                                     ap("ibf", BF16, 128, 128),
                                     ap("sc_dph", BF16, 128, 512),
                                     start=True, stop=False)
                            for hp in range(8):
                                ins = t.matmul(pbs[2][:, hp * 64:(hp + 1) * 64],
                                               ap("sc_mt", BF16, 128, 128,
                                                  col0=hp * 128),
                                               ap("sc_hdbig", BF16, 128, 64,
                                                  col0=(h0 + hp) * 64),
                                               start=False, stop=(hp == 7))
                            sPE.inc(ins)
                            if not (g == 0 and si == 0):
                                prev_t1 = (DVc(c, 10 + 17 * g) if si == 1
                                           else DVc(c, 17 + 17 * (g - 1)))
                                t.wait_ge(sDV.h, prev_t1)
                            ins = t.matmul(pbs[3][:, 0:512],
                                           ap("sc_cfm", BF16, 128, 128,
                                              col0=g * 128),
                                           ap("sc_pbf", BF16, 128, 512,
                                              col0=h0 * 64),
                                           start=True, stop=True)
                            sPE.inc(ins)
                    for f in range(16):
                        bank = pbs[[4, 6, 1, 5][f % 4]]
                        if f == 0:
                            t.wait_ge(sDV.h, DVc(c, 143))
                        elif f == 1:
                            t.wait_ge(sAC.h, ACc(c, 3))
                        elif f >= 4:
                            t.wait_ge(sDV.h, DVc(c, 141 + f))
                        for hh in range(8):
                            hd_ = f * 8 + hh
                            ins = t.matmul(
                                bank[:, hh * 64:(hh + 1) * 64],
                                ap("sc_btm", BF16, 128, 128,
                                   col0=(hd_ // 16) * 128),
                                ap("sc_hd2big", BF16, 128, 64, col0=hd_ * 64),
                                start=True, stop=True)
                        sPE.inc(ins)

            @block.scalar
            def _(s):
                for c in range(NCH):
                    if c == 0:
                        gX.wait(s, t_pst[0])
                        s.wait_ge(sDV.h, d0)
                    else:
                        s.wait_ge(sDV.h, DVc(c - 1, 160))
                        s.wait_ge(sPE.h, PEc(c - 1, 61))
                    ins = s.copy(ap("sc_pbf", BF16, 128, DI),
                                 ap("pst", F32, 128, DI))
                    sAC.inc(ins)
                    s.wait_ge(sPE.h, PEc(c, 2))
                    ins = s.activation(ap("sc_dec", F32, 128, 128),
                                       pbs[7][:, 0:128], AF.Exp)
                    sAC.inc(ins)
                    s.wait_ge(sPE.h, PEc(c, 3))
                    ins = s.activation(ap("sc_etb", F32, 128, 128),
                                       pbs[6][:, 0:128], AF.Exp)
                    sAC.inc(ins)
                    s.wait_ge(sDV.h, DVc(c, 2))
                    ins = s.activation(ap("sc_eacs", F32, 128, 128),
                                       ap("sc_acs", F32, 128, 128), AF.Exp)
                    sAC.inc(ins)
                    for g in range(8):
                        for si in range(2):
                            h0 = 16 * g + 8 * si
                            s.wait_ge(sDV.h, DVc(c, 7 + 17 * g + 7 * si))
                            ins = s.activation(ap("sc_dsb", F32, 128, 1024),
                                               ap("sc_dsb", F32, 128, 1024),
                                               AF.Relu, scale=-1.0)
                            sAC.inc(ins)
                            s.wait_ge(sAC.h, sAC.n)
                            ins = s.activation(ap("sc_dsb", F32, 128, 1024),
                                               ap("sc_dsb", F32, 128, 1024),
                                               AF.Exp, scale=-1.0)
                            sAC.inc(ins)
                            if g == 0 and si == 0:
                                gTD.wait(s, gt[c])
                            else:
                                prev_yg = (DVc(c, 12 + 17 * g) if si == 1
                                           else DVc(c, 19 + 17 * (g - 1)))
                                s.wait_ge(sDV.h, prev_yg)
                            ins = s.activation(ap("sc_sg", F32, 128, 512),
                                               ap("sc_gate", BF16, 128, 512,
                                                  col0=h0 * 64), AF.Silu)
                            sAC.inc(ins)
                        s.wait_ge(sDV.h, DVc(c, 21 + 17 * g))
                        ins = s.activation(cc(13), cc(12), AF.Ln,
                                           bias=ap("epsc", F32, 128, 1),
                                           scale=1.0 / 1024.0)
                        sAC.inc(ins)
                        s.wait_ge(sAC.h, sAC.n)
                        ins = s.activation(cc(13), cc(13), AF.Exp, scale=-0.5)
                        sAC.inc(ins)

            @block.vector
            def _(v):
                for c in range(NCH):
                    v.wait_ge(sPE.h, PEc(c, 1))
                    v.tensor_copy(ap("sc_datm", F32, 128, 128), pbs[6][:, 0:128])
                    ins = v.tensor_copy(ap("sc_dttm", F32, 128, 128),
                                        pbs[7][:, 0:128])
                    sDV.inc(ins)
                    v.wait_ge(sPE.h, PEc(c, 2))
                    if c >= 1:
                        v.wait_ge(sAC.h, ACc(c - 1, 4))
                    ins = v.tensor_copy(ap("sc_acs", F32, 128, 128),
                                        pbs[6][:, 0:128])
                    sDV.inc(ins)
                    v.wait_ge(sPE.h, PEc(c, 4))
                    if c >= 1:
                        gSP.wait(v, tok_acs[c - 1])
                    ins = v.tensor_copy(ap("sc_acsf", F32, 128, 128),
                                        pbs[7][:, 0:128])
                    sDV.inc(ins)
                    v.wait_ge(sPE.h, PEc(c, 5))
                    ins = v.tensor_copy(ap("sc_btm", BF16, 128, 1024),
                                        pbs[0].ap().bitcast(BF16)[:, 0:1024])
                    sDV.inc(ins)
                    gTD.wait(v, ht[c])
                    v.wait_ge(sDV.h, DVc(c, 1))
                    ins = v.tensor_tensor(h3(ap("sc_hdbig", BF16, 128, DI)),
                                          h3(ap("sc_hid", BF16, 128, DI)),
                                          w3(ap("sc_dttm", F32, 128, 128)),
                                          ALU.mult)
                    sDV.inc(ins)
                    for g in range(8):
                        v.wait_ge(sPE.h, PEc(c, 6 + 7 * g))
                        ins = v.tensor_tensor(ap("sc_gm", F32, 128, 128),
                                              pbs[7][:, 0:128],
                                              ap("ctr", F32, 128, 128), ALU.mult)
                        sDV.inc(ins)
                        for si in range(2):
                            h0 = 16 * g + 8 * si
                            v.wait_ge(sPE.h, PEc(c, 7 + 7 * g + 3 * si))
                            for q in range(2):
                                ins = v.scalar_tensor_tensor(
                                    ap("sc_dsb", F32, 128, 512,
                                       col0=512 * q).rearrange(
                                           "p (h i) -> p h i", i=128),
                                    pbs[1 + 4 * q][:, 0:512].rearrange(
                                        "p (h i) -> p h i", i=128),
                                    1.0,
                                    ap("sc_acs", F32, 128, 4,
                                       col0=h0 + 4 * q).rearrange(
                                           "p (h o) -> p h o",
                                           o=1).broadcast_to([128, 4, 128]),
                                    ALU.mult, ALU.subtract)
                            sDV.inc(ins)
                            v.wait_ge(sAC.h, ACc(c, 6 + 8 * g + 3 * si))
                            ins = v.scalar_tensor_tensor(
                                ap("sc_mt", BF16, 128, 1024).rearrange(
                                    "p (h i) -> p h i", i=128),
                                ap("sc_dsb", F32, 128, 1024).rearrange(
                                    "p (h i) -> p h i", i=128),
                                1.0,
                                ap("sc_gm", F32, 128, 128).rearrange(
                                    "p (a i) -> p a i",
                                    a=1).broadcast_to([128, 8, 128]),
                                ALU.min, ALU.mult)
                            sDV.inc(ins)
                            if not (g == 0 and si == 0):
                                prev_yd = (PEc(c, 8 + 7 * g) if si == 1
                                           else PEc(c, 4 + 7 * g))
                                v.wait_ge(sPE.h, prev_yd)
                            ins = v.tensor_tensor(
                                h3(ap("sc_dph", BF16, 128, 512)),
                                h3(ap("sc_hid", BF16, 128, 512, col0=h0 * 64)),
                                w3(ap("dpbc", F32, 128, 8, col0=h0), 8),
                                ALU.mult)
                            sDV.inc(ins)
                            v.wait_ge(sPE.h, PEc(c, 9 + 7 * g + 3 * si))
                            ins = v.tensor_tensor(
                                h3(ap("sc_t1", F32, 128, 512)),
                                pbs[3][:, 0:512].rearrange("p (h o) -> p h o",
                                                           o=64),
                                w3(ap("sc_eacs", F32, 128, 8, col0=h0), 8),
                                ALU.mult)
                            sDV.inc(ins)
                            v.wait_ge(sDV.h, DVc(c, 10 + 17 * g + 7 * si))
                            v.wait_ge(sPE.h, PEc(c, 8 + 7 * g + 3 * si))
                            ins = v.tensor_tensor(ap("sc_t1", F32, 128, 512),
                                                  pbs[2][:, 0:512],
                                                  ap("sc_t1", F32, 128, 512),
                                                  ALU.add)
                            sDV.inc(ins)
                            v.wait_ge(sAC.h, ACc(c, 7 + 8 * g + 3 * si))
                            v.wait_ge(sDV.h, DVc(c, 11 + 17 * g + 7 * si))
                            ins = v.tensor_tensor(
                                ap("sc_yg", F32, 128, 512, col0=512 * si),
                                ap("sc_t1", F32, 128, 512),
                                ap("sc_sg", F32, 128, 512), ALU.mult)
                            sDV.inc(ins)
                            v.wait_ge(sDV.h, DVc(c, 12 + 17 * g + 7 * si))
                            ins = v.scalar_tensor_tensor(
                                ap("sc_sqj", F32, 128, 512),
                                ap("sc_yg", F32, 128, 512, col0=512 * si), 1.0,
                                ap("sc_yg", F32, 128, 512, col0=512 * si),
                                ALU.mult, ALU.mult, accum_out=cc(10 + si))
                            sDV.inc(ins)
                        v.wait_ge(sDV.h, DVc(c, 20 + 17 * g))
                        ins = v.tensor_tensor(cc(12), cc(10), cc(11), ALU.add)
                        sDV.inc(ins)
                        v.wait_ge(sAC.h, ACc(c, 12 + 8 * g))
                        if g == 0 and c >= 1:
                            gSP.wait(v, tok_y2[c - 1])
                        ins = v.tensor_scalar(
                            ap("sc_y2", BF16, 128, 1024, col0=1024 * g),
                            ap("sc_yg", F32, 128, 1024), cc(13), None, ALU.mult)
                        sDV.inc(ins)
                    v.wait_ge(sAC.h, ACc(c, 2))
                    ins = v.tensor_tensor(ap("sc_w2", F32, 128, 128),
                                          ap("sc_dttm", F32, 128, 128),
                                          ap("sc_dec", F32, 128, 128), ALU.mult)
                    sDV.inc(ins)
                    v.wait_ge(sDV.h, DVc(c, 142))
                    ins = v.tensor_tensor(h3(ap("sc_hd2big", BF16, 128, DI)),
                                          h3(ap("sc_hid", BF16, 128, DI)),
                                          w3(ap("sc_w2", F32, 128, 128)),
                                          ALU.mult)
                    sDV.inc(ins)
                    v.wait_ge(sAC.h, ACc(c, 3))
                    ins = v.tensor_tensor(h3(ap("pst", F32, 128, DI)),
                                          h3(ap("pst", F32, 128, DI)),
                                          w3(ap("sc_etb", F32, 128, 128)),
                                          ALU.mult)
                    sDV.inc(ins)
                    v.wait_ge(sDV.h, DVc(c, 144))
                    for f in range(16):
                        bank = pbs[[4, 6, 1, 5][f % 4]]
                        v.wait_ge(sPE.h, PEc(c, 62 + f))
                        ins = v.tensor_tensor(
                            ap("pst", F32, 128, 512, col0=f * 512),
                            ap("pst", F32, 128, 512, col0=f * 512),
                            bank[:, 0:512], ALU.add)
                        sDV.inc(ins)

        def emit_outproj():
            snap = snapshot()
            pe0 = sPE.n
            d0 = sDV.n
            wt = []
            yt = [[[None] * 64 for _ in range(2)] for _ in range(8)]
            spt = [[[None] * 8 for _ in range(2)] for _ in range(8)]

            def dv_fold(oc):
                return int(d0 + 17 * oc + 1)

            def dv_evac(oc, tb, j):
                return int(d0 + 17 * oc + 1 + 8 * tb + j + 1)

            def pe_mm(oc, tb, kt):
                return int(pe0 + 128 * oc + 64 * tb + kt + 1)

            @block.gpsimd
            def _(g_):
                wait_snap(g_, snap)
                for oc in range(8):
                    if oc >= 1:
                        g_.wait_ge(sPE.h, int(pe0 + 128 * oc))
                    wt.append(gW.dma(
                        g_, out=ap("op_w", BF16, 128, 64 * 512),
                        in_=wopT[:, oc * 512:(oc + 1) * 512].rearrange(
                            "(kt p) o -> p kt o", p=128)))
                    for tb in range(2):
                        for j in range(8):
                            g_.wait_ge(sDV.h, dv_evac(oc, tb, j))
                            spt[oc][tb][j] = gSP.dma(
                                g_, out=mo_scr[tb * 1024 + j * 128:
                                               tb * 1024 + (j + 1) * 128,
                                               oc * 512:(oc + 1) * 512],
                                in_=ap(f"op_s{j % 2}", F32, 128, 512))

            @block.sync
            def _(sy):
                wait_snap(sy, snap)
                for oc in range(8):
                    for tb in range(2):
                        for kt in range(64):
                            if not (oc == 0 and tb == 0 and kt < 2):
                                prev = 128 * oc + 64 * tb + kt - 2
                                sy.wait_ge(sPE.h, int(pe0 + prev + 1))
                            yt[oc][tb][kt] = gTD.dma(
                                sy, out=ap(f"op_y{'ab'[kt % 2]}", BF16, 128,
                                           1024),
                                in_=y2_scr[tb * 1024:(tb + 1) * 1024,
                                           kt * 128:(kt + 1) * 128],
                                transpose=True)

            @block.tensor
            def _(t):
                for oc in range(8):
                    for tb in range(2):
                        for kt in range(64):
                            gTD.wait(t, yt[oc][tb][kt])
                            if kt == 0:
                                t.wait_ge(sDV.h, dv_fold(oc))
                                if oc or tb:
                                    pb_ = (oc * 2 + tb) - 1
                                    t.wait_ge(sDV.h, dv_evac(pb_ // 2,
                                                             pb_ % 2, 7))
                            for tt in range(8):
                                ins = t.matmul(
                                    pbs[tt][:, 0:512],
                                    ap(f"op_y{'ab'[kt % 2]}", BF16, 128, 128,
                                       col0=tt * 128),
                                    ap("op_w", BF16, 128, 512, col0=kt * 512),
                                    start=(kt == 0), stop=(kt == 63))
                            sPE.inc(ins)

            @block.vector
            def _(v):
                for oc in range(8):
                    gW.wait(v, wt[oc])
                    ins = v.tensor_tensor(
                        ap("op_w", BF16, 128, 64 * 512).rearrange(
                            "p (k o) -> p k o", o=512),
                        ap("op_w", BF16, 128, 64 * 512).rearrange(
                            "p (k o) -> p k o", o=512),
                        ap("gnw", F32, 128, 64).rearrange(
                            "p (h o) -> p h o", o=1).broadcast_to(
                                [128, 64, 512]),
                        ALU.mult)
                    sDV.inc(ins)
                    for tb in range(2):
                        for j in range(8):
                            v.wait_ge(sPE.h, pe_mm(oc, tb, 63))
                            e = oc * 16 + tb * 8 + j
                            if e >= 2:
                                pj = e - 2
                                gSP.wait(v, spt[pj // 16][(pj % 16) // 8]
                                         [pj % 8])
                            ins = v.tensor_copy(ap(f"op_s{j % 2}", F32, 128,
                                                   512), pbs[j][:, 0:512])
                            sDV.inc(ins)

        def emit_epilogue():
            d0 = sDV.n
            a0 = sAC.n
            xts = []
            mts = []
            uts = [None] * 16
            vts = [None] * 16

            def cc(i, rr=128):
                return ap("col", F32, rr, 1, col0=4 * i)

            @block.sync
            def _(sy):
                barrier(sy)
                for i in range(16):
                    if i >= 2:
                        sy.wait_ge(sDV.h, int(d0 + 17 * (i - 2) + 12))
                    xts.append(gX.dma(sy, out=ap(f"me_x{i % 2}", F32, 128, DM),
                                      in_=x_own[PFX + i * 128:
                                                PFX + (i + 1) * 128, :]))
                    if i >= 2:
                        sy.wait_ge(sDV.h, int(d0 + 17 * (i - 2) + 11))
                    mts.append(gX.dma(sy, out=ap(f"me_mo{i % 2}", F32, 128,
                                                 DM),
                                      in_=mo_scr[i * 128:(i + 1) * 128, :]))

            @block.gpsimd
            def _(g_):
                for i in range(16):
                    g_.wait_ge(sDV.h, int(d0 + 17 * i + 13))
                    uts[i] = gSP.dma(g_, out=u_scr[i * 128:(i + 1) * 128, :],
                                     in_=ap("me_u", F32, 128, DM))
                    g_.wait_ge(sDV.h, int(d0 + 17 * i + 17))
                    vts[i] = gSP.dma(g_, out=v_scr[i * 128:(i + 1) * 128, :],
                                     in_=ap("me_v", BF16, 128, DM))

            @block.scalar
            def _(s):
                for i in range(16):
                    DP = d0 + 17 * i
                    for ln in range(3):
                        src = [ap(f"me_x{i % 2}", F32, 128, DM),
                               ap(f"me_mo{i % 2}", F32, 128, DM),
                               ap("me_u", F32, 128, DM)][ln]
                        s.wait_ge(sDV.h, int(DP + [1, 7, 13][ln]))
                        ins = s.activation(ap("me_j", BF16, 128, DM), src,
                                           AF.Square, accum_out=cc(1))
                        sAC.inc(ins)
                        s.wait_ge(sDV.h, int(DP + [3, 9, 15][ln]))
                        ins = s.activation(cc(5), cc(4), AF.Ln,
                                           bias=ap("epsc", F32, 128, 1))
                        sAC.inc(ins)
                        s.wait_ge(sAC.h, sAC.n)
                        ins = s.activation(cc(5), cc(5), AF.Exp, scale=-0.5)
                        sAC.inc(ins)

            @block.vector
            def _(v):
                for i in range(16):
                    DP = d0 + 17 * i
                    AP_ = a0 + 9 * i
                    xv = ap(f"me_x{i % 2}", F32, 128, DM)
                    mv = ap(f"me_mo{i % 2}", F32, 128, DM)
                    uv = ap("me_u", F32, 128, DM)
                    gX.wait(v, xts[i])
                    for ln in range(3):
                        src = [xv, mv, uv][ln]
                        if ln == 1:
                            gX.wait(v, mts[i])
                            v.wait_ge(sDV.h, int(DP + 6))
                        if ln == 2:
                            v.wait_ge(sDV.h, int(DP + 12))
                        ins = v.tensor_scalar(src, src, 1.0, None, ALU.mult,
                                              ALU.add, accum_out=cc(0))
                        sDV.inc(ins)
                        v.wait_ge(sAC.h, int(AP_ + 3 * ln + 1))
                        v.tensor_tensor(cc(3), cc(0), cc(0), ALU.mult)
                        ins = v.tensor_scalar(cc(2), cc(1), 1.0 / DM, None,
                                              ALU.mult)
                        sDV.inc(ins)
                        v.wait_ge(sDV.h, sDV.n)
                        ins = v.scalar_tensor_tensor(cc(4), cc(3),
                                                     -1.0 / (DM * DM), cc(2),
                                                     ALU.mult, ALU.add)
                        sDV.inc(ins)
                        v.wait_ge(sAC.h, int(AP_ + 3 * ln + 3))
                        ins = v.tensor_scalar(cc(6), cc(5), cc(0), 1.0 / DM,
                                              ALU.mult, ALU.mult)
                        sDV.inc(ins)
                        v.wait_ge(sDV.h, sDV.n)
                        if ln == 2:
                            if i >= 1:
                                gSP.wait(v, vts[i - 1])
                            ins = v.tensor_scalar(ap("me_v", BF16, 128, DM),
                                                  uv, cc(5), cc(6),
                                                  ALU.mult, ALU.subtract)
                            sDV.inc(ins)
                        else:
                            if ln == 0 and i >= 1:
                                gSP.wait(v, uts[i - 1])
                            ins = v.tensor_scalar(uv, src, cc(5), cc(6),
                                                  ALU.mult, ALU.subtract)
                            sDV.inc(ins)
                            v.wait_ge(sDV.h, sDV.n)
                            if ln == 0:
                                gX.wait(v, mts[i])
                                ins = v.tensor_tensor(mv, mv, uv, ALU.add)
                            else:
                                ins = v.tensor_tensor(uv, uv, xv, ALU.add)
                            sDV.inc(ins)

        def emit_ffn1():
            pe0 = sPE.n
            d0 = sDV.n
            specs = []
            for j in range(31):
                specs.append((vhT[:, j * 128:(j + 1) * 128], 128, "stail", j,
                              mid_scr[j * 128:(j + 1) * 128, :]))
            specs.append((vhT[:, 3968:4032], 64, "stail", 31,
                          mid_scr[3968:4032, :]))
            specs.append((vhtopT, 64, "shead", 0, mid_scr[4032:4096, :]))
            specs.append((vhrT, 4, "sdc", 0, mdl_scr))
            nt = len(specs)
            vt = []
            wt = []
            spt = [None] * nt

            @block.sync
            def _(sy):
                barrier(sy)
                for kt in range(32):
                    vt.append(gTD.dma(
                        sy, out=ap("vfm", BF16, 128, T, col0=kt * T),
                        in_=v_scr[:, kt * 128:(kt + 1) * 128], transpose=True))

            @block.gpsimd
            def _(g_):
                gTD.wait_all(g_)
                for j, (wsrc, rr, snm, scol, dst) in enumerate(specs):
                    if j >= 2:
                        g_.wait_ge(sPE.h, int(pe0 + 4 * (j - 1)))
                    wt.append(gW.dma(
                        g_, out=ap(f"f1_w{j % 2}", BF16, 128, 32 * rr),
                        in_=wsrc.rearrange("(kt p) r -> p kt r", p=128)))
                    if j >= 1:
                        pj = j - 1
                        g_.wait_ge(sDV.h, int(d0 + 4 * pj + 4))
                        rrp = specs[pj][1]
                        spt[pj] = gSP.dma(g_, out=specs[pj][4],
                                          in_=ap(f"f1_md{pj % 2}", BF16, rrp,
                                                 T))
                j = nt - 1
                g_.wait_ge(sDV.h, int(d0 + 4 * j + 4))
                spt[j] = gSP.dma(g_, out=specs[j][4],
                                 in_=ap(f"f1_md{j % 2}", BF16, specs[j][1], T))

            @block.tensor
            def _(t):
                gTD.wait_all(t)
                m = 0
                for j, (wsrc, rr, snm, scol, dst) in enumerate(specs):
                    gW.wait(t, wt[j])
                    for tc in range(4):
                        bank = pbs[m % 4]
                        if m >= 4:
                            t.wait_ge(sDV.h, int(d0 + m - 3))
                        for kt in range(32):
                            ins = t.matmul(
                                bank[0:rr, 0:512],
                                ap(f"f1_w{j % 2}", BF16, 128, rr,
                                   col0=kt * rr),
                                ap("vfm", BF16, 128, 512,
                                   col0=kt * T + tc * 512),
                                start=(kt == 0), stop=(kt == 31))
                        sPE.inc(ins)
                        m += 1

            @block.vector
            def _(v):
                m = 0
                for j, (wsrc, rr, snm, scol, dst) in enumerate(specs):
                    for tc in range(4):
                        v.wait_ge(sPE.h, int(pe0 + m + 1))
                        if tc == 0 and j >= 2 and spt[j - 2] is not None:
                            gSP.wait(v, spt[j - 2])
                        ins = v.tensor_scalar(
                            ap(f"f1_md{j % 2}", BF16, rr, 512,
                               col0=tc * 512),
                            pbs[m % 4][0:rr, 0:512],
                            ap(snm, F32, rr, 1, col0=scol), None, ALU.mult)
                        sDV.inc(ins)
                        m += 1

        def emit_ffn2():
            snap = snapshot()
            pe0 = sPE.n
            d0 = sDV.n
            mt = []
            bt = [None] * 8
            wts = [[None] * 3 for _ in range(8)]
            spt = [[None] * 16 for _ in range(8)]

            @block.gpsimd
            def _(g_):
                wait_snap(g_, snap)
                for oc in range(8):
                    if oc >= 1:
                        g_.wait_ge(sPE.h, int(pe0 + 16 * oc))
                    wts[oc][0] = gW.dma(
                        g_, out=ap("f2_ut", BF16, 128, 31 * 512),
                        in_=utT[0:3968, oc * 512:(oc + 1) * 512].rearrange(
                            "(kt p) o -> p kt o", p=128))
                    wts[oc][1] = gW.dma(
                        g_, out=ap("f2_ut", BF16, 64, 512, col0=31 * 512),
                        in_=utT[3968:4032, oc * 512:(oc + 1) * 512])
                    wts[oc][2] = gW.dma(
                        g_, out=ap("f2_ut", BF16, 64, 512, row0=64,
                                   col0=31 * 512),
                        in_=uttopT_in[:, oc * 512:(oc + 1) * 512])
                    for tt in range(16):
                        g_.wait_ge(sDV.h, int(d0 + 16 * oc + tt + 1))
                        spt[oc][tt] = gSP.dma(
                            g_, out=ffn_scr[tt * 128:(tt + 1) * 128,
                                            oc * 512:(oc + 1) * 512],
                            in_=ap(f"f2_s{['t', '1'][tt % 2]}"
                                   if False else ("f2_st" if tt % 2 == 0
                                                  else "f2_s1"),
                                   F32, 128, 512))

            @block.sync
            def _(sy):
                wait_snap(sy, snap)
                for kt in range(32):
                    mt.append(gX.dma(
                        sy, out=ap("mid", BF16, 128, T, col0=kt * T),
                        in_=mid_scr[kt * 128:(kt + 1) * 128, :]))
                mt.append(gX.dma(sy, out=ap("f2_mdl", BF16, 4, T),
                                 in_=mdl_scr))
                for oc in range(8):
                    if oc >= 1:
                        sy.wait_ge(sPE.h, int(pe0 + 16 * oc))
                    bt[oc] = gX.dma(sy, out=ap("f2_bb", F32, 1, 512),
                                    in_=fbias_in[:, oc * 512:(oc + 1) * 512])

            @block.tensor
            def _(t):
                for tok in mt:
                    gX.wait(t, tok)
                for oc in range(8):
                    for j in range(3):
                        gW.wait(t, wts[oc][j])
                    gX.wait(t, bt[oc])
                    for tt in range(16):
                        m = oc * 16 + tt
                        if m >= 4:
                            t.wait_ge(sDV.h, int(d0 + m - 3))
                        bank = pbs[tt % 4]
                        for kt in range(32):
                            t.matmul(bank[:, 0:512],
                                     ap("mid", BF16, 128, 128,
                                        col0=kt * T + tt * 128),
                                     ap("f2_ut", BF16, 128, 512,
                                        col0=kt * 512),
                                     start=(kt == 0), stop=False)
                        t.matmul(bank[:, 0:512],
                                 ap("f2_mdl", BF16, 4, 128, col0=tt * 128),
                                 ap("w2t", BF16, 4, 512, col0=oc * 512),
                                 start=False, stop=False)
                        ins = t.matmul(bank[:, 0:512],
                                       ap("c1r", F32, 1, 128),
                                       ap("f2_bb", F32, 1, 512),
                                       start=False, stop=True)
                        sPE.inc(ins)

            @block.vector
            def _(v):
                for oc in range(8):
                    for tt in range(16):
                        m = oc * 16 + tt
                        v.wait_ge(sPE.h, int(pe0 + m + 1))
                        if m >= 2 and spt[(m - 2) // 16][(m - 2) % 16]:
                            gSP.wait(v, spt[(m - 2) // 16][(m - 2) % 16])
                        ins = v.tensor_copy(
                            ap("f2_st" if tt % 2 == 0 else "f2_s1", F32, 128,
                               512), pbs[tt % 4][:, 0:512])
                        sDV.inc(ins)

        def emit_final():
            snap = snapshot()
            d0 = sDV.n
            a0 = sAC.n
            fts = []
            uts = []
            ots = [None] * 16

            def cc(i, rr=128):
                return ap("col", F32, rr, 1, col0=4 * i)

            @block.gpsimd
            def _(g_):
                for i in range(16):
                    g_.wait_ge(sDV.h, int(d0 + 6 * i + 6))
                    ots[i] = gOUT.dma(g_, out=out_t[i * 128:(i + 1) * 128, :],
                                      in_=ap(f"fn_f{i % 2}", F32, 128, DM))

            @block.sync
            def _(sy):
                wait_snap(sy, snap)
                for i in range(16):
                    if i >= 2:
                        sy.wait_ge(sDV.h, int(d0 + 6 * (i - 2) + 6))
                        gOUT.wait(sy, ots[i - 2])
                    fts.append(gX.dma(sy, out=ap(f"fn_f{i % 2}", F32, 128,
                                                 DM),
                                      in_=ffn_scr[i * 128:(i + 1) * 128, :]))
                    uts.append(gX.dma(sy, out=ap(f"fn_u{i % 2}", F32, 128,
                                                 DM),
                                      in_=u_scr[i * 128:(i + 1) * 128, :]))

            @block.scalar
            def _(s):
                for i in range(16):
                    DP = d0 + 6 * i
                    s.wait_ge(sDV.h, int(DP + 1))
                    ins = s.activation(ap("fn_j", BF16, 128, DM),
                                       ap(f"fn_f{i % 2}", F32, 128, DM),
                                       AF.Square, accum_out=cc(1))
                    sAC.inc(ins)
                    s.wait_ge(sDV.h, int(DP + 3))
                    ins = s.activation(cc(5), cc(4), AF.Ln,
                                       bias=ap("epsc", F32, 128, 1))
                    sAC.inc(ins)
                    s.wait_ge(sAC.h, sAC.n)
                    ins = s.activation(cc(5), cc(5), AF.Exp, scale=-0.5)
                    sAC.inc(ins)

            @block.vector
            def _(v):
                for i in range(16):
                    DP = d0 + 6 * i
                    AP_ = a0 + 3 * i
                    fv = ap(f"fn_f{i % 2}", F32, 128, DM)
                    uv = ap(f"fn_u{i % 2}", F32, 128, DM)
                    gX.wait(v, fts[i])
                    ins = v.tensor_scalar(fv, fv, 1.0, None, ALU.mult,
                                          ALU.add, accum_out=cc(0))
                    sDV.inc(ins)
                    v.wait_ge(sAC.h, int(AP_ + 1))
                    v.tensor_tensor(cc(3), cc(0), cc(0), ALU.mult)
                    ins = v.tensor_scalar(cc(2), cc(1), 1.0 / DM, None,
                                          ALU.mult)
                    sDV.inc(ins)
                    v.wait_ge(sDV.h, sDV.n)
                    ins = v.scalar_tensor_tensor(cc(4), cc(3),
                                                 -1.0 / (DM * DM), cc(2),
                                                 ALU.mult, ALU.add)
                    sDV.inc(ins)
                    v.wait_ge(sAC.h, int(AP_ + 3))
                    ins = v.tensor_scalar(cc(6), cc(5), cc(0), 1.0 / DM,
                                          ALU.mult, ALU.mult)
                    sDV.inc(ins)
                    v.wait_ge(sDV.h, sDV.n)
                    ins = v.tensor_scalar(fv, fv, cc(5), cc(6),
                                          ALU.mult, ALU.subtract)
                    sDV.inc(ins)
                    v.wait_ge(sDV.h, sDV.n)
                    gX.wait(v, uts[i])
                    ins = v.tensor_tensor(fv, fv, uv, ALU.add)
                    sDV.inc(ins)
                    v.wait_ge(sDV.h, sDV.n)

        with nc.named_scope("p0_const"):
            phase0()
        if upto >= 1:
            with nc.named_scope("p1_hprep_dup"):
                emit_hprep(x_dup)
        if upto >= 2:
            with nc.named_scope("p2_inproj_dup"):
                emit_inproj(OT_HID + OT_B + OT_DT)
        if upto >= 3:
            with nc.named_scope("p3_chain"):
                emit_chain()
        if upto >= 4:
            with nc.named_scope("p4_hprep_own"):
                emit_hprep(x_own)
        if upto >= 5:
            with nc.named_scope("p5_inproj_own"):
                emit_inproj(OT_HID + OT_B + OT_C + OT_GATE + OT_DT)
        if upto >= 6:
            with nc.named_scope("p6_scan"):
                emit_scan()
        if upto >= 7:
            with nc.named_scope("p7_outproj"):
                emit_outproj()
        if upto >= 8:
            with nc.named_scope("p8_epilogue"):
                emit_epilogue()
        if upto >= 9:
            with nc.named_scope("p9_ffn1"):
                emit_ffn1()
        if upto >= 10:
            with nc.named_scope("p10_ffn2"):
                emit_ffn2()
        if upto >= 11:
            with nc.named_scope("p11_final"):
                emit_final()

        @block.gpsimd
        def _(g):
            barrier(g)
            if debug:
                for nm, s_ in [("d_h", h_scr), ("d_hid", hid_scr), ("d_b", b_scr)]:
                    gOUT.dma(g, out=dbg_outs[nm], in_=s_)
                gOUT.dma(g, out=dbg_outs["d_nwbc"], in_=ap("nwbc", F32, 128, DM))
                if upto >= 2:
                    gOUT.dma(g, out=dbg_outs["d_dt"], in_=ap("dtfm", F32, 128, T))
                    gOUT.dma(g, out=dbg_outs["d_da"], in_=ap("dafm", F32, 128, T))
                if upto >= 3:
                    gOUT.dma(g, out=dbg_outs["d_p"], in_=p_scr)
                if upto >= 5:
                    gOUT.dma(g, out=dbg_outs["d_c"], in_=c_scr)
                    gOUT.dma(g, out=dbg_outs["d_gate"], in_=gate_scr)
                if upto >= 6:
                    gOUT.dma(g, out=dbg_outs["d_y2"], in_=y2_scr)
                if upto >= 7:
                    gOUT.dma(g, out=dbg_outs["d_mo"], in_=mo_scr)
                if upto >= 8:
                    gOUT.dma(g, out=dbg_outs["d_u"], in_=u_scr)
                    gOUT.dma(g, out=dbg_outs["d_v"], in_=v_scr)
                if upto >= 9:
                    gOUT.dma(g, out=dbg_outs["d_mid"], in_=mid_scr)
                    gOUT.dma(g, out=dbg_outs["d_mdl"], in_=mdl_scr)
                if upto >= 10:
                    gOUT.dma(g, out=dbg_outs["d_ffn"], in_=ffn_scr)
            gOUT.wait_all(g)

    return nc


_NC_CACHE = {}


def _get_nc(debug, upto):
    key = (debug, upto)
    if key not in _NC_CACHE:
        _NC_CACHE[key] = build(debug=debug, upto=upto)
    return _NC_CACHE[key]


def make_in_maps(inputs):
    x = np.ascontiguousarray(inputs["L0_attn_x"], np.float32)
    shared = dict(
        wipT=np.ascontiguousarray(inputs["L0_attn_in_proj_w"].T),
        wopT=np.ascontiguousarray(inputs["L0_attn_out_proj_w"].T),
        conv_w_r=np.ascontiguousarray(
            inputs["L0_attn_conv_w"].reshape(80, 128, 4).transpose(1, 0, 2)
            .reshape(128, 320)),
        conv_b_r=np.ascontiguousarray(
            inputs["L0_attn_conv_b"].reshape(80, 128).T),
        dt_bias=np.ascontiguousarray(inputs["L0_attn_dt_bias"].reshape(128, 1)),
        a_log=np.ascontiguousarray(inputs["L0_attn_A_log"].reshape(128, 1)),
        dp_row=np.ascontiguousarray(inputs["L0_attn_Dp"].reshape(1, 128)),
        gnw_col=np.ascontiguousarray(
            inputs["L0_attn_gnorm_w"].reshape(64, 128).T),
        nw_row=np.ascontiguousarray(inputs["L0_attn_norm_w"].reshape(1, DM)),
        vhT=np.ascontiguousarray(inputs["L0_ffn_Vh_tail"].T),
        utT=np.ascontiguousarray(inputs["L0_ffn_U_tail"].T),
        vhtopT=np.ascontiguousarray(inputs["L0_ffn_Vh_top"].T),
        uttopT=np.ascontiguousarray(inputs["L0_ffn_U_top"].T),
        vhrT=np.ascontiguousarray(inputs["L0_ffn_Vh_tail_r"].T),
        urT=np.ascontiguousarray(inputs["L0_ffn_U_tail_r"].T),
        r_mat=np.ascontiguousarray(inputs["L0_ffn_R"]),
        stail_col=np.ascontiguousarray(
            np.pad(inputs["L0_ffn_S_tail"], (0, 64)).reshape(32, 128).T),
        stop_col=np.ascontiguousarray(inputs["L0_ffn_S_top"].reshape(64, 1)),
        alpha_col=np.ascontiguousarray(inputs["L0_ffn_alpha"].reshape(64, 1)),
        beta_col=np.ascontiguousarray(inputs["L0_ffn_beta"].reshape(64, 1)),
        str_col=np.ascontiguousarray(inputs["L0_ffn_S_tail_r"].reshape(4, 1)),
        dvec_col=np.ascontiguousarray(inputs["L0_ffn_D"].reshape(4, 1)),
        fbias_row=np.ascontiguousarray(inputs["L0_ffn_bias"].reshape(1, DM)),
    )
    in_maps = []
    for core in range(8):
        bi, half = core // 2, core % 2
        t0 = half * T
        xo = np.zeros((TE, DM), np.float32)
        xo[PFX:] = x[bi, t0:t0 + T]
        if t0 >= PFX:
            xo[:PFX] = x[bi, t0 - PFX:t0]
        xd = np.zeros((TE, DM), np.float32)
        xd[PFX:] = x[bi, 0:T]
        m = dict(shared)
        m["x_own"] = xo
        m["x_dup"] = xd
        m["mask_in"] = np.full((128, 1), float(half), np.float32)
        in_maps.append(m)
    return in_maps


def kernel(**inputs):
    nc = _get_nc(DEBUG, BUILD_UPTO)
    in_maps = make_in_maps(inputs)
    res = run_bass_kernel_spmd(nc, in_maps, core_ids=list(range(8)))
    out = np.zeros((4, 4096, DM), np.float32)
    for core in range(8):
        bi, half = core // 2, core % 2
        out[bi, half * T:(half + 1) * T] = res.results[core]["out"]
    return out
